# version 12 (final)
# baseline (speedup 1.0000x reference)
import math
import numpy as np
from contextlib import ExitStack

import concourse.bass as bass
import concourse.mybir as mybir

F32 = mybir.dt.float32
BF16 = mybir.dt.bfloat16
AF = mybir.ActivationFunctionType
ALU = mybir.AluOpType
AX = mybir.AxisListType

S, D, NH, NKV, HD = 2048, 2048, 32, 8, 64
QKV = (NH + 2 * NKV) * HD
L = 1024
EHID = [1024, 2048, 3072, 4096, 1024, 2048, 3072, 4096]
NSEQ = 2
T = NSEQ * S
DC, LCH = D // 128, L // 128
TWO_PI, PI = 2.0 * math.pi, math.pi
EPS = 1e-5
COST_LAMBDA = 1e-7
ROPE_BASE = 10000.0
NEG_LN_BASE_16 = -math.log(ROPE_BASE) / 16.0


def host_constants():
    con = {}
    con["ident_f32"] = np.eye(128, dtype=np.float32)
    perm = np.zeros((128, 128), dtype=np.float32)
    for m in range(128):
        lm = m % 64
        b = m - lm
        src = b + lm + 16 if lm < 16 else (b + lm - 16 if lm < 32 else m)
        perm[src, m] = 1.0
    con["rope_perm"] = perm
    ks = np.arange(128)[:, None]
    qs = np.arange(512)[None, :]
    con["causal_masks"] = np.concatenate(
        [(ks <= a + qs).astype(np.float32) for a in (0, -128, -256, -384)], axis=1)
    j = np.arange(16, dtype=np.float64)
    invfreq = 1.0 / ROPE_BASE ** (2 * j / 32)
    t = np.arange(S, dtype=np.float64)
    theta = t[None, :] * invfreq[:, None]
    con["n_sin"] = np.round(theta / TWO_PI).astype(np.float32)
    con["n_cos"] = np.round((theta + PI / 2) / TWO_PI).astype(np.float32)
    con["router_cost"] = np.array([[L * h + h * L for h in EHID]], dtype=np.float32)
    selc = np.zeros((16, 128), dtype=np.float32)
    sel1 = np.zeros((1, 128), dtype=np.float32)
    sels = np.zeros((16, 128), dtype=np.float32)
    for r in range(128):
        lm = r % 64
        if lm < 32:
            selc[lm % 16, r] = 1.0
            sels[lm % 16, r] = -1.0 if lm < 16 else 1.0
        else:
            sel1[0, r] = 1.0
    con["rope_selc"] = selc
    con["rope_sel1"] = sel1
    con["rope_sels"] = sels
    return con


class P:
    ENGS = ("sync", "gpsimd", "vector", "scalar", "tensor")

    def __init__(self, nc, ctx):
        self.nc, self.ctx = nc, ctx
        self.q = {e: [] for e in self.ENGS}
        self.sems, self.cnt = {}, {}
        self.freed = {}

    def sem(self, name):
        h = self.ctx.enter_context(self.nc.semaphore(name))
        self.sems[name], self.cnt[name] = h, 0
        return h

    def on(self, eng):
        def deco(fn):
            self.q[eng].append(fn)
        return deco

    def inc(self, name, by=1):
        self.cnt[name] += by
        return self.cnt[name]

    def free_at(self, buf, semname, value):
        self.freed[buf] = (semname, value)

    def guard(self, buf):
        return self.freed.get(buf, (None, 0))


class _TDef:

    def __init__(self, px):
        self.px = px

    def then_inc(self, sem, n):
        px = self.px
        if px._last is not None and not px._used:
            px._used = True
            return px._last.then_inc(sem, n)
        return px._e.drain().then_inc(sem, n)


class TProxy:
    def __init__(self, e):
        self._e, self._last, self._used = e, None, False

    def __getattr__(self, k):
        return getattr(self._e, k)

    def matmul(self, *a, **kw):
        r = self._e.matmul(*a, **kw)
        self._last, self._used = r, False
        return r

    def transpose(self, *a, **kw):
        r = self._e.transpose(*a, **kw)
        self._last, self._used = r, False
        return r

    def drain(self):
        return _TDef(self)


def add_p12(p, SEM, R, PH):
    nc = p.nc
    din, sb, scr = R["din"], R["sb"], R["scr"]
    W_, WG, barrier, DBG, SB = R["helpers"]
    emit_pc, pc_left, PC_TOTAL = R["precast"]
    x_in, wqkv, wout_w = din["x"], din["Wqkv"], din["Wout"]
    scr_q, scr_xmid, scr_rows = scr["q"], scr["xmid"], scr["rows"]
    ident, identb, permb = sb["ident"], sb["identb"], sb["permb"]
    masksb, ropeC, ropeS = sb["masksb"], sb["ropeC"], sb["ropeS"]
    onesb, onesf = sb["onesb"], sb["onesf"]
    modv, bqkv_fm, bkfm = sb["modv"], sb["bqkv_fm"], sb["bkfm"]
    ln2ab, bvB, boutr, bcastA = sb["ln2ab"], sb["bvB"], sb["boutr"], sb["bcastA"]
    cpool = sb["cpool"]

    def W(e, s, v):
        if v > 0:
            e.wait_ge(SEM[s], v)

    do_p2 = "p2" in PH

    stKV = ExitStack()
    Khat = stKV.enter_context(nc.sbuf_tensor("Khat", [65, NKV * S], BF16))
    Vhat = stKV.enter_context(nc.sbuf_tensor("Vhat", [128, 16 * 520], BF16))
    vst = stKV.enter_context(nc.sbuf_tensor("vst", [128, 1], F32))
    ainv = stKV.enter_context(nc.sbuf_tensor("ainv", [128, 1], F32))
    bneg = stKV.enter_context(nc.sbuf_tensor("bneg", [128, 1], F32))
    stt_ = stKV.enter_context(nc.sbuf_tensor("stt", [128, 24], F32))
    agg = stKV.enter_context(nc.sbuf_tensor("agg", [128, 2], F32))

    pp = ExitStack()
    pT2 = [pp.enter_context(nc.psum_tensor(f"pT2{i}", [128, 1024], F32)) for i in range(2)]
    pS = [pp.enter_context(nc.psum_tensor(f"pS{i}", [128, 512], F32)) for i in range(2)]
    pO2 = [pp.enter_context(nc.psum_tensor(f"pO2{i}", [128, 512], F32)) for i in range(2)]

    def emit_seq(s_):
        soff = s_ * 4 * DC

        st1 = ExitStack()
        T1 = lambda n, shp, dt: st1.enter_context(nc.sbuf_tensor(n + str(s_), list(shp), dt))
        tb = T1("tb", (128, D), F32)
        hfm = T1("hfm", (128, DC * S), BF16)
        wband = [T1(f"wband{i}", (128, DC * 128), BF16) for i in range(2)]
        wk32 = T1("wk32", (128, DC * 64), F32)
        wvb = T1("wvb", (128, DC * 512), BF16)
        xb = [T1("xb0", (128, D), F32), wvb[:, 0:2 * D].bitcast(F32)]
        qraw = T1("qraw", (128, 512), BF16)
        qt2 = T1("qt2", (128, 512), BF16)
        qfin = T1("qfin", (128, 512), BF16)

        ev_done = p.cnt["vv"]
        xg = {0: 0, 1: 0}
        for i in range(16):
            slot = i % 2
            ssl = "xs0" if slot == 0 else "xs1"

            @p.on("sync")
            def f(e, s_=s_, i=i, ssl=ssl, slot=slot, g=xg[slot]):
                if g:
                    W(e, "vv", g)
                e.dma_start(out=xb[slot][:], in_=x_in[s_, i * 128:(i + 1) * 128, :])\
                 .then_inc(SEM[ssl], 16)
            xtgt = p.inc(ssl, 16)

            @p.on("vector")
            def f(e, slot=slot, ssl=ssl, xtgt=xtgt, xb=xb):
                W(e, ssl, xtgt)
                for k4 in range(4):
                    e.bn_stats(stt_[:, k4 * 6:(k4 + 1) * 6],
                               xb[slot][:, k4 * 512:(k4 + 1) * 512])
                e.drain()
                e.bn_aggr(agg[:], stt_[:])
                e.drain()
                e.tensor_scalar(vst[:], agg[:, 1:2], EPS, None, op0=ALU.add)
                e.drain()
                e.drain().then_inc(SEM["vv"], 1)
            vv_st = p.inc("vv")

            @p.on("scalar")
            def f(e, vv_st=vv_st):
                W(e, "vv", vv_st)
                e.activation(vst[:], vst[:], AF.Sqrt).then_inc(SEM["sc"], 1)
            sc_sq = p.inc("sc")

            @p.on("vector")
            def f(e, slot=slot, sc_sq=sc_sq, xb=xb, tb=tb):
                W(e, "sc", sc_sq)
                e.reciprocal(ainv[:], vst[:])
                e.tensor_scalar(bneg[:], agg[:, 0:1], -1.0, None, op0=ALU.mult)
                e.drain()
                e.tensor_scalar(tb[:], xb[slot][:], bneg[:], ainv[:],
                                op0=ALU.add, op1=ALU.mult)
                e.drain()
                e.drain().then_inc(SEM["vv"], 1)
            vv_nm = p.inc("vv")
            xg[slot] = vv_nm

            @p.on("tensor")
            def f(e, vv_nm=vv_nm, evg=ev_done, tb=tb):
                W(e, "vv", max(vv_nm, evg))
                for kc in range(DC):
                    e.transpose(pT2[kc // 8][0:128, (kc % 8) * 128:(kc % 8) * 128 + 128],
                                tb[:, kc * 128:(kc + 1) * 128], ident[:])
                e.drain().then_inc(SEM["tv"], 1)
            tv_tr = p.inc("tv")

            @p.on("vector")
            def f(e, i=i, soff=soff, tv_tr=tv_tr, hfm=hfm):
                W(e, "tv", tv_tr)
                g512, tin = divmod(i, 4)
                for kc in range(DC):
                    base = kc * S + g512 * 512 + tin * 128
                    e.tensor_scalar(hfm[:, base:base + 128],
                                    pT2[kc // 8][0:128, (kc % 8) * 128:(kc % 8) * 128 + 128],
                                    modv[:, soff + DC + kc:soff + DC + kc + 1],
                                    modv[:, soff + kc:soff + kc + 1],
                                    op0=ALU.mult, op1=ALU.add)
                e.drain()
                e.drain().then_inc(SEM["vv"], 1)
            ev_done = p.inc("vv")

        barrier()
        d = DBG(f"hfm{s_}", (128, DC * S), BF16)
        if d is not None:
            @p.on("sync")
            def f(e, d=d, hfm=hfm):
                e.dma_start(out=d, in_=hfm[:]).then_inc(SEM["so"], 16)
            p.inc("so", 16)

        @p.on("vector")
        def f(e):
            e.memset(Khat[64:65, :], -1.0)
            e.memset(Vhat[:].rearrange("p (t h j) -> p t h j", h=NKV, j=65)
                     [:, :, :, 64:65], 1.0)
            e.drain().then_inc(SEM["vv"], 1)
        p.inc("vv")

        psfree = [p.cnt["vv"], p.cnt["vv"]]
        ps2free = p.cnt["vv"]
        wslot_guard = [0, 0]
        wk32_guard = 0
        q_store_guard = 0
        for ci in range(24):
            isq = ci < 16
            slot = ci % 2
            band, ssem = wband[slot], ("ws0" if slot == 0 else "ws1")
            if isq:
                colbeg, colw = ci * 128, 128
            else:
                kvh = ci - 16
                colbeg, colw = 2048 + kvh * 64, 64

            @p.on("gpsimd")
            def f(e, colbeg=colbeg, colw=colw, band=band, ssem=ssem, isq=isq,
                  g=wslot_guard[slot], g2=wk32_guard, wk32=wk32):
                W(e, "tv", g)
                if not isq:
                    W(e, "vv", g2)
                dst = (band[:, 0:DC * colw] if isq else wk32[:, 0:DC * colw])\
                    .rearrange("p (c f) -> p c f", c=DC)
                e.dma_start(out=dst,
                            in_=wqkv[:, colbeg:colbeg + colw]
                            .rearrange("(c p) f -> p c f", p=128)).then_inc(SEM[ssem], 16)
            wtgt = p.inc(ssem, 16)
            emit_pc(1)
            kv = None
            if not isq:
                @p.on("vector")
                def f(e, band=band, ssem=ssem, wtgt=wtgt, colw=colw, wk32=wk32):
                    W(e, ssem, wtgt)
                    e.tensor_scalar(band[:, 0:DC * colw], wk32[:, 0:DC * colw],
                                    0.125, None, op0=ALU.mult)
                    e.drain()
                    e.drain().then_inc(SEM["vv"], 1)
                kv = p.inc("vv")
                wk32_guard = kv

            for grp in range(4):
                g = grp % 2
                @p.on("tensor")
                def f(e, grp=grp, g=g, band=band, ssem=ssem, wtgt=wtgt, kv=kv,
                      colw=colw, fr=psfree[g], hfm=hfm):
                    if grp == 0:
                        W(e, ssem, wtgt)
                        if kv is not None:
                            W(e, "vv", kv)
                    W(e, "vv", fr)
                    for kc in range(DC):
                        e.matmul(pS[g][0:colw, 0:512],
                                 band[:, kc * colw:(kc + 1) * colw],
                                 hfm[:, kc * S + grp * 512:kc * S + grp * 512 + 512],
                                 start=(kc == 0), stop=(kc == DC - 1))
                    e.drain().then_inc(SEM["tv"], 1)
                tv_mm = p.inc("tv")
                if grp == 3:
                    wslot_guard[slot] = tv_mm

                @p.on("vector")
                def f(e, ci=ci, g=g, tv_mm=tv_mm, colw=colw, isq=isq, qraw=qraw):
                    W(e, "tv", tv_mm)
                    bias = (bqkv_fm[:, ci:ci + 1] if isq
                            else bkfm[:, ci - 16:ci - 16 + 1])
                    e.tensor_scalar(qraw[0:colw, :], pS[g][0:colw, 0:512], bias,
                                    None, op0=ALU.add)
                    e.drain()
                    e.drain().then_inc(SEM["vv"], 1)
                vv_ev = p.inc("vv")
                psfree[g] = vv_ev

                @p.on("tensor")
                def f(e, vv_ev=vv_ev, colw=colw, fr=ps2free, qraw=qraw):
                    W(e, "vv", max(vv_ev, fr))
                    e.matmul(pO2[0][0:colw, 0:512], permb[0:colw, 0:colw],
                             qraw[0:colw, 0:512], start=True, stop=True)
                    e.drain().then_inc(SEM["tv"], 1)
                tv_pm = p.inc("tv")

                @p.on("vector")
                def f(e, ci=ci, grp=grp, tv_pm=tv_pm, colw=colw, isq=isq,
                      qraw=qraw, qt2=qt2, qfin=qfin, qsg=q_store_guard):
                    W(e, "tv", tv_pm)
                    if qsg:
                        W(e, "qs0", qsg)
                    e.tensor_copy(qt2[0:colw, :], pO2[0][0:colw, 0:512])
                    e.drain()
                    e.tensor_tensor(qt2[0:colw, :], qt2[0:colw, :],
                                    ropeS[0:colw, grp * 512:(grp + 1) * 512], op=ALU.mult)
                    e.drain()
                    e.tensor_tensor(qfin[0:colw, :], qraw[0:colw, :],
                                    ropeC[0:colw, grp * 512:(grp + 1) * 512], op=ALU.mult)
                    e.drain()
                    if isq:
                        e.tensor_tensor(qfin[0:colw, :], qfin[0:colw, :],
                                        qt2[0:colw, :], op=ALU.add)
                    else:
                        kvh = ci - 16
                        base = kvh * S + grp * 512
                        e.tensor_tensor(Khat[0:64, base:base + 512],
                                        qfin[0:64, :], qt2[0:64, :], op=ALU.add)
                    e.drain()
                    e.drain().then_inc(SEM["vv"], 1)
                vv_rp = p.inc("vv")
                ps2free = vv_rp

                if isq:
                    @p.on("sync")
                    def f(e, ci=ci, grp=grp, vv_rp=vv_rp, qfin=qfin):
                        W(e, "vv", vv_rp)
                        for hh in range(2):
                            h = ci * 2 + hh
                            e.dma_start(
                                out=scr_q[0:64, h * S + grp * 512:h * S + grp * 512 + 512],
                                in_=qfin[hh * 64:(hh + 1) * 64, :]).then_inc(SEM["qs0"], 16)
                    q_store_guard = p.inc("qs0", 32)

        @p.on("gpsimd")
        def f(e, g=max(wslot_guard), wvb=wvb):
            W(e, "tv", g)
            e.dma_start(out=wvb[:].rearrange("p (c f) -> p c f", c=DC),
                        in_=wqkv[:, 2560:3072].rearrange("(c p) f -> p c f", p=128))\
             .then_inc(SEM["w20"], 16)
        w2tgt = p.inc("w20", 16)

        for tc in range(16):
            g = tc % 2
            @p.on("tensor")
            def f(e, tc=tc, g=g, w2tgt=w2tgt, fr=psfree[g], hfm=hfm, wvb=wvb):
                if tc == 0:
                    W(e, "w20", w2tgt)
                W(e, "vv", fr)
                for kc in range(DC):
                    e.matmul(pS[g][0:128, 0:512],
                             hfm[:, kc * S + tc * 128:kc * S + tc * 128 + 128],
                             wvb[:, kc * 512:(kc + 1) * 512],
                             start=(kc == 0), stop=(kc == DC - 1))
                e.drain().then_inc(SEM["tv"], 1)
            tv_v = p.inc("tv")

            @p.on("vector")
            def f(e, tc=tc, g=g, tv_v=tv_v):
                W(e, "tv", tv_v)
                dst = Vhat[:, tc * 520:(tc + 1) * 520]\
                    .rearrange("p (h j) -> p h j", j=65)[:, :, 0:64]
                src = pS[g][0:128, 0:512].rearrange("p (h j) -> p h j", j=64)
                bb = bvB[:].rearrange("p (h j) -> p h j", j=64)
                e.tensor_tensor(dst, src, bb, op=ALU.add)
                e.drain()
                e.drain().then_inc(SEM["vv"], 1)
            psfree[g] = p.inc("vv")

        barrier()
        st1.close()
        d = DBG(f"khat{s_}", (65, NKV * S), BF16)
        if d is not None:
            @p.on("sync")
            def f(e, d=d):
                e.dma_start(out=d, in_=Khat[:]).then_inc(SEM["so"], 16)
            p.inc("so", 16)
        d = DBG(f"vhat{s_}", (128, 16 * 520), BF16)
        if d is not None:
            @p.on("sync")
            def f(e, d=d):
                e.dma_start(out=d, in_=Vhat[:]).then_inc(SEM["so"], 16)
            p.inc("so", 16)

        if not do_p2:
            return

        st2 = ExitStack()
        T2 = lambda n, shp, dt: st2.enter_context(nc.sbuf_tensor(n + str(s_), list(shp), dt))
        Qq = T2("Qq", (65, NH * 512), BF16)
        Ofm = T2("Ofm", (128, DC * 512), BF16)
        u16 = T2("u16", (128, DC * 512), BF16)
        u16f = u16[:].bitcast(F32)
        PT = [u16[:, 0:1024], u16[:, 1024:2048]]
        PTm = u16[:, 2048:3072]
        otmp = [u16[0:64, 3072:3584], u16[0:64, 3584:4096]]
        absq = [u16[0:64, 7168:7680], u16[0:64, 7680:8192]]
        rtmp = u16f[:, 2048:2560]
        rzB = [u16f[0:64, 2560:3072], u16f[0:64, 3072:3584]]
        rzs = [T2(f"rzs{i}", (1, 512), F32) for i in range(2)]
        cmax = T2("cmax", (64, 8), BF16)
        wob = T2("wob", (128, DC * 512), BF16)
        xmb = T2("xmb", (128, 4 * D), F32)

        @p.on("sync")
        def f(e, s_=s_):
            e.dma_start(out=bcastA[:],
                        in_=scr_rows[s_ * 2, 0:2048].unsqueeze(0)
                        .partition_broadcast(128)).then_inc(SEM["sx"], 16)
        sx_ga = p.inc("sx", 16)

        @p.on("vector")
        def f(e):
            for kvh in range(NKV):
                e.reduce_max(cmax[:, kvh:kvh + 1], Khat[0:64, kvh * S:(kvh + 1) * S],
                             axis=AX.X, apply_absolute_value=True)
            e.drain()
            e.drain().then_inc(SEM["vv"], 1)
        vv_cm = p.inc("vv")

        pofree = {0: p.cnt["vv"], 1: p.cnt["vv"]}
        otg = {0: 0, 1: 0}
        qq_g = [p.cnt["tv"]]
        WOL = [0]
        mh_mm = {0: 0, 1: 0}
        mh_ev = {0: 0, 1: 0}
        for qg in range(4):
            @p.on("sync")
            def f(e, s_=s_, qg=qg, g=qq_g[0]):
                W(e, "tv", g)
                e.dma_start(out=Qq[0:64, :].rearrange("k (h s) -> k h s", s=512),
                            in_=scr_q.rearrange("k (h s) -> k h s", h=NH)
                            [:, :, qg * 512:(qg + 1) * 512]).then_inc(SEM["sx"], 16)
            sx_q = p.inc("sx", 16)

            for h in range(NH):
                kvh = h // 4
                sl = h % 2
                @p.on("scalar")
                def f(e, h=h, sl=sl, sx_q=sx_q, g=mh_mm[sl]):
                    if h == 0:
                        W(e, "sx", sx_q)
                    if g:
                        W(e, "tv", g)
                    e.activation(absq[sl][:], Qq[0:64, h * 512:(h + 1) * 512],
                                 AF.Abs).then_inc(SEM["sc"], 1)
                sc_a = p.inc("sc")

                @p.on("tensor")
                def f(e, kvh=kvh, sl=sl, sc_a=sc_a, vv_cm=vv_cm,
                      fr=(ps2free if h < 2 else mh_ev[sl])):
                    W(e, "sc", sc_a)
                    W(e, "vv", max(vv_cm, fr))
                    e.matmul(pT2[0][0:1, sl * 512:(sl + 1) * 512],
                             cmax[:, kvh:kvh + 1], absq[sl][:],
                             start=True, stop=True)
                    e.drain().then_inc(SEM["tv"], 1)
                mh_mm[sl] = p.inc("tv")

                @p.on("vector")
                def f(e, h=h, sl=sl, tv_mh=mh_mm[sl]):
                    W(e, "tv", tv_mh)
                    e.tensor_copy(Qq[64:65, h * 512:(h + 1) * 512],
                                  pT2[0][0:1, sl * 512:(sl + 1) * 512])
                    e.drain().then_inc(SEM["vv"], 1)
                mh_ev[sl] = p.inc("vv")

            mh_done = p.cnt["vv"]

            npair = 2 * (qg + 1)
            exp_done = {0: 0, 1: 0}
            pend_tail = [None]

            def emit_scores(h, cp, mh_done=mh_done):
                par = cp % 2
                @p.on("tensor")
                def f(e, h=h, cp=cp, kvh=h // 4, par=par, g1=exp_done[par],
                      vvq=mh_done):
                    W(e, "sc", g1)
                    W(e, "vv", vvq)
                    for j in range(2):
                        kb = cp * 2 + j
                        e.matmul(pT2[par][0:128, j * 512:(j + 1) * 512],
                                 Khat[0:65, kvh * S + kb * 128:kvh * S + kb * 128 + 128],
                                 Qq[0:65, h * 512:(h + 1) * 512],
                                 start=True, stop=True)
                        e.drain().then_inc(SEM["tv"], 1)
                return (p.inc("tv"), p.inc("tv"))

            for h in range(NH):
                kvh = h // 4
                po = pO2[h % 2]
                av_done = {0: 0, 1: 0}
                tv_s = {0: emit_scores(h, 0)}
                prev_vv_m = 0
                for cp in range(npair):
                    par = cp % 2
                    diag = (cp >= npair - 2)
                    sc_es = []
                    for j in range(2):
                        @p.on("scalar")
                        def f(e, j=j, tv_sc=tv_s[cp][j], par=par, diag=diag,
                              g=av_done[par], PT=PT, PTm=PTm,
                              vg=prev_vv_m if cp == npair - 1 and j == 0 else 0):
                            W(e, "tv", max(tv_sc, g))
                            if vg:
                                W(e, "vv", vg)
                            dst = PTm if diag else PT[par]
                            e.activation(dst[:, j * 512:(j + 1) * 512],
                                         pT2[par][0:128, j * 512:(j + 1) * 512],
                                         AF.Exp).then_inc(SEM["sc"], 1)
                        sc_es.append(p.inc("sc"))
                    sc_e = sc_es[1]
                    exp_done[par] = sc_e

                    vv_m = None
                    if diag:
                        jo = 0 if cp == npair - 2 else 2
                        @p.on("vector")
                        def f(e, par=par, jo=jo, sc_e=sc_e, g=av_done[par],
                              PT=PT, PTm=PTm):
                            W(e, "sc", sc_e)
                            W(e, "tv", g)
                            for j in range(2):
                                e.tensor_tensor(PT[par][:, j * 512:(j + 1) * 512],
                                                PTm[:, j * 512:(j + 1) * 512],
                                                masksb[:, (jo + j) * 512:(jo + j + 1) * 512],
                                                op=ALU.mult)
                            e.drain().then_inc(SEM["vv"], 1)
                        vv_m = p.inc("vv")
                        prev_vv_m = vv_m

                    if cp + 1 < npair:
                        tv_s[cp + 1] = emit_scores(h, cp + 1)

                    for j in range(2):
                        @p.on("tensor")
                        def f(e, cp=cp, npair=npair, kvh=kvh, par=par, po=po,
                              j=j, scw=sc_es[j], vv_m=vv_m, frO=pofree[h % 2],
                              h=h, PT=PT):
                            if vv_m is not None:
                                if j == 0:
                                    W(e, "vv", vv_m)
                            else:
                                W(e, "sc", scw)
                            if cp == 0 and j == 0:
                                W(e, "vv", frO)
                            kb = cp * 2 + j
                            e.matmul(po[0:65, 0:512],
                                     Vhat[:, kb * 520 + kvh * 65:kb * 520 + kvh * 65 + 65],
                                     PT[par][:, j * 512:(j + 1) * 512],
                                     start=(cp == 0 and j == 0),
                                     stop=(cp == npair - 1 and j == 1))
                            e.drain().then_inc(SEM["tv"], 1)
                        p.inc("tv")
                    av_done[par] = p.cnt["tv"]

                    if cp == 1 and pend_tail[0] is not None:
                        pend_tail[0]()
                        pend_tail[0] = None

                hs2 = h % 2
                @p.on("vector")
                def f(e, po=po, hs2=hs2, g=max(av_done.values())):
                    W(e, "tv", g)
                    e.reciprocal(rzs[hs2][:], po[64:65, 0:512])
                    e.drain().then_inc(SEM["vv"], 1)
                vv_rz = p.inc("vv")

                def tail(h=h, hs2=hs2, po=po, vv_rz=vv_rz):
                    nonlocal ps2free
                    @p.on("tensor")
                    def f(e, hs2=hs2, vv_rz=vv_rz, fr=psfree[hs2]):
                        W(e, "vv", max(vv_rz, fr))
                        e.matmul(pS[hs2][0:64, 0:512], onesf[0:1, 0:64], rzs[hs2][:],
                                 start=True, stop=True)
                        e.drain().then_inc(SEM["tv"], 1)
                    tv_rb = p.inc("tv")

                    @p.on("vector")
                    def f(e, h=h, hs2=hs2, po=po, tv_rb=tv_rb, Ofm=Ofm,
                          go=otg[hs2], wog=WOL[0]):
                        W(e, "tv", max(tv_rb, wog))
                        W(e, "sx", go)
                        e.tensor_copy(rzB[hs2][:], pS[hs2][0:64, 0:512])
                        e.drain()
                        if h % 2 == 0:
                            e.tensor_tensor(Ofm[0:64, (h // 2) * 512:(h // 2) * 512 + 512],
                                            po[0:64, 0:512], rzB[hs2][:], op=ALU.mult)
                        else:
                            e.tensor_tensor(otmp[hs2][:], po[0:64, 0:512], rzB[hs2][:],
                                            op=ALU.mult)
                        e.drain().then_inc(SEM["vv"], 1)
                    vv_oe = p.inc("vv")
                    ps2free = vv_oe
                    pofree[h % 2] = vv_oe
                    psfree[hs2] = vv_oe

                    if h % 2 == 1:
                        @p.on("sync")
                        def f(e, h=h, hs2=hs2, vv_oe=vv_oe, Ofm=Ofm):
                            W(e, "vv", vv_oe)
                            e.dma_start(out=Ofm[64:128, (h // 2) * 512:(h // 2) * 512 + 512],
                                        in_=otmp[hs2][:]).then_inc(SEM["sx"], 16)
                        sx_ot = p.inc("sx", 16)
                        otg[hs2] = sx_ot

                pend_tail[0] = tail

            pend_tail[0]()
            pend_tail[0] = None
            qq_g[0] = p.cnt["tv"]

            @p.on("vector")
            def f(e, g=max(otg.values()) if otg else 0):
                W(e, "sx", g)
                e.drain().then_inc(SEM["vv"], 1)
            p.inc("vv")

            ofm_ready = p.cnt["vv"]

            @p.on("sync")
            def f(e, s_=s_, qg=qg, g=p.cnt["vv"], gq=p.cnt["qs0"], xmb=xmb):
                W(e, "vv", g)
                W(e, "qs0", gq)
                e.dma_start(out=xmb[:].rearrange("p (t d) -> p t d", t=4),
                            in_=x_in[s_, qg * 512:(qg + 1) * 512, :]
                            .rearrange("(t p) d -> p t d", p=128)).then_inc(SEM["sx"], 16)
            sx_x = p.inc("sx", 16)

            woguard = WOL[0]
            for dg in range(4):
                @p.on("gpsimd")
                def f(e, dg=dg, g=woguard, wob=wob):
                    W(e, "tv", g)
                    e.dma_start(out=wob[:, :].rearrange("p (c f) -> p c f", c=DC),
                                in_=wout_w[:, dg * 512:(dg + 1) * 512]
                                .rearrange("(c p) f -> p c f", p=128))\
                     .then_inc(SEM["w21"], 16)
                wotgt = p.inc("w21", 16)

                for tcl in range(4):
                    g = tcl % 2
                    @p.on("tensor")
                    def f(e, dg=dg, tcl=tcl, g=g, wotgt=wotgt, fr=psfree[g],
                          ofr=ofm_ready, Ofm=Ofm, wob=wob):
                        if tcl == 0:
                            W(e, "w21", wotgt)
                            W(e, "vv", ofr)
                        W(e, "vv", fr)
                        for fc in range(DC):
                            e.matmul(pS[g][0:128, 0:512],
                                     Ofm[:, fc * 512 + tcl * 128:fc * 512 + tcl * 128 + 128],
                                     wob[:, fc * 512:(fc + 1) * 512],
                                     start=(fc == 0), stop=False)
                        e.matmul(pS[g][0:128, 0:512], onesb[0:1, 0:128],
                                 boutr[0:1, dg * 512:(dg + 1) * 512],
                                 start=False, stop=True)
                        e.drain().then_inc(SEM["tv"], 1)
                    tv_w = p.inc("tv")
                    woguard = tv_w
                    WOL[0] = tv_w

                    @p.on("vector")
                    def f(e, dg=dg, tcl=tcl, g=g, tv_w=tv_w, sx_ga=sx_ga,
                          sx_x=sx_x, xmb=xmb, rtmp=rtmp):
                        W(e, "tv", tv_w)
                        W(e, "sx", max(sx_ga, sx_x))
                        e.tensor_tensor(rtmp[:], pS[g][0:128, 0:512],
                                        bcastA[:, dg * 512:(dg + 1) * 512], op=ALU.mult)
                        e.drain()
                        base = tcl * D + dg * 512
                        e.tensor_tensor(xmb[:, base:base + 512],
                                        xmb[:, base:base + 512], rtmp[:], op=ALU.add)
                        e.drain()
                        e.drain().then_inc(SEM["vv"], 1)
                    psfree[g] = p.inc("vv")

            for tcl in range(4):
                tile = s_ * 16 + qg * 4 + tcl

                @p.on("vector")
                def f(e, tcl=tcl, g=p.cnt["vv"], xmb=xmb):
                    W(e, "vv", g)
                    for k4 in range(4):
                        e.bn_stats(stt_[:, k4 * 6:(k4 + 1) * 6],
                                   xmb[:, tcl * D + k4 * 512:tcl * D + (k4 + 1) * 512])
                    e.drain()
                    e.bn_aggr(agg[:], stt_[:])
                    e.drain()
                    e.tensor_scalar(vst[:], agg[:, 1:2], EPS, None, op0=ALU.add)
                    e.drain()
                    e.drain().then_inc(SEM["vv"], 1)
                vv_st = p.inc("vv")

                @p.on("scalar")
                def f(e, vv_st=vv_st):
                    W(e, "vv", vv_st)
                    e.activation(vst[:], vst[:], AF.Sqrt).then_inc(SEM["sc"], 1)
                sc_sq = p.inc("sc")

                @p.on("vector")
                def f(e, tile=tile, sc_sq=sc_sq):
                    W(e, "sc", sc_sq)
                    e.reciprocal(ln2ab[:, 2 * tile + 1:2 * tile + 2], vst[:])
                    e.tensor_scalar(ln2ab[:, 2 * tile:2 * tile + 1], agg[:, 0:1],
                                    -1.0, None, op0=ALU.mult)
                    e.drain()
                    e.drain().then_inc(SEM["vv"], 1)
                vv_ab = p.inc("vv")

                @p.on("sync")
                def f(e, s_=s_, qg=qg, tcl=tcl, vv_ab=vv_ab, xmb=xmb):
                    W(e, "vv", vv_ab)
                    tok = s_ * S + qg * 512 + tcl * 128
                    e.dma_start(out=scr_xmid[tok:tok + 128, :],
                                in_=xmb[:, tcl * D:(tcl + 1) * D]).then_inc(SEM["qs0"], 16)
            p.inc("qs0", 16 * 4)

        barrier()
        st2.close()

    for _s in range(NSEQ):
        emit_seq(_s)
    emit_pc(10 ** 9)

    d = DBG("xmid", (T, D))
    if d is not None:
        @p.on("sync")
        def f(e, d=d):
            e.dma_start(out=d, in_=scr_xmid).then_inc(SEM["so"], 16)
        p.inc("so", 16)

    stKV.close()
    pp.close()
    R["sb12"].close()
    R["p2state"] = {}


def add_p345(p, SEM, R):
    nc = p.nc
    din, sb, scr = R["din"], R["sb"], R["scr"]
    W_, WG, barrier, DBG, SB = R["helpers"]
    out = R["out"]
    scr_xmid, scr_core, scr_rows, scr_c10 = scr["xmid"], scr["core"], scr["rows"], scr["c10"]
    down_w, up_w, core_w = din["down_w"], din["up_w"], din["core_w"]
    shared_w1, shared_w2 = scr["s1b"], scr["s2b"]
    emit_pc, pc_left, PC_TOTAL = R["precast"]
    ew1 = scr["e1b"]
    eb1 = [din[f"e{e}_b1"] for e in range(8)]
    ew2 = scr["e2b"]
    ident = sb["ident"]
    onesb, onesf = sb["onesb"], sb["onesf"]
    modv, dbfm, sb1fm = sb["modv"], sb["dbfm"], sb["sb1fm"]
    ln2ab, bcastA, ubcbr = sb["ln2ab"], sb["bcastA"], sb["ubcbr"]
    rwsb, gb1sb = sb["rwsb"], sb["gb1sb"]
    b2s, sb2r = sb["b2s"], sb["sb2r"]

    def W(e, s, v):
        if v > 0:
            e.wait_ge(SEM[s], v)

    def WT(e, *guards):
        for g in guards:
            if g is not None and g[1] > 0:
                e.wait_ge(SEM[g[0]], g[1])

    st = ExitStack()
    T_ = lambda n, shp, dt: st.enter_context(nc.sbuf_tensor(n, list(shp), dt))
    U = T_("U", (128, 32768), BF16)
    Uf = U[:].bitcast(F32)
    xmb = [Uf[:, 0:2048], Uf[:, 2048:4096]]
    h2 = U[:, 8192:16384]
    g2 = U[:, 16384:24576]
    cwb = U[:, 24576:32768]
    hid = U[:, 0:32768]
    gbufs = [T_("gbuf", (128, LCH * 1024), BF16),
             T_("gbuf2", (128, LCH * 1024), BF16)]
    moe = T_("moe", (128, LCH * 1024), F32)
    wA = T_("wA", (128, 4096), BF16)
    wB = T_("wB", (128, 4096), BF16)
    wC = T_("wC", (128, 4096), BF16)
    wD = T_("wD", (128, 4096), BF16)
    w1s = [wA[:, 0:4096], wB[:, 0:4096]]
    w2s = [wC[:, 0:4096], wD[:, 0:4096]]
    dwn = wB[:, 0:2048]
    ups2 = [wC[:, 0:4096], wD[:, 0:4096]]
    ctmp = T_("ctmp", (128, 512), F32)
    xms = [T_(f"xms{i}", (128, 512), F32)[:] for i in range(2)]
    cbs = [T_(f"cbs{i}", (128, 512), F32)[:] for i in range(2)]
    fins = [T_(f"fins{i}", (128, 512), F32)[:] for i in range(2)]
    webb = T_("webb", (128, 1024), BF16)
    b1f = T_("b1f", (128, 32), F32)
    ltmp = T_("ltmp", (8, 512), F32)
    rbadjB = T_("rbadjB", (128, 8), F32)
    rm = T_("rm", (128, 1), F32)
    rz_ = T_("rz_", (128, 1), F32)
    rex = T_("rex", (128, 8), F32)
    srt = T_("srt", (128, 8), F32)
    rg0 = T_("rg0", (128, 1), F32)
    rg1 = T_("rg1", (128, 1), F32)
    rt0 = T_("rt0", (128, 8), F32)
    w10s = T_("w10s", (128, 10), F32)
    c10sb = T_("c10sb", (10, L), BF16)
    w10fm = T_("w10fm", (10, 1024), BF16)
    logit_s = T_("logit_s", (128, 8 * 8), F32)
    ct8 = Uf[0:8, 6144:7168]
    ctrow = Uf[0:1, 7168:8192]
    one8 = T_("one8", (8, 1), F32)

    pp = ExitStack()
    pZ = [pp.enter_context(nc.psum_tensor(f"pZ{i}", [128, 1024], F32)) for i in range(2)]
    pW = [pp.enter_context(nc.psum_tensor(f"pW{i}", [128, 512], F32)) for i in range(4)]

    @p.on("sync")
    def f(e, w=p.cnt["sx"] + 16):
        e.dma_start(out=scr_rows[15:16, 0:8], in_=sb["rbadj"][:]).then_inc(SEM["sx"], 16)
        e.wait_ge(SEM["sx"], w)
        e.dma_start(out=rbadjB[:], in_=scr_rows[15, 0:8].unsqueeze(0)
                    .partition_broadcast(128)).then_inc(SEM["sx"], 16)
    p.inc("sx", 16)
    sx_rbB = p.inc("sx", 16)

    @p.on("vector")
    def f(e):
        e.memset(one8[:], 1.0).then_inc(SEM["vv"], 1)
    p.inc("vv")

    p5xf = {0: 0, 1: 0}
    p5fso = {0: 0, 1: 0}
    p5upt = {}
    PG = {i: None for i in range(4)}
    GBF = {0: 0, 1: 0}
    SXGB = {}

    def emit_blk(blk, c10_ready, p5q):
        s_ = blk // 2
        soff = s_ * 4 * DC
        gbuf = gbufs[blk % 2]
        moebf = gbuf
        pg = PG
        zg = {i: ("sc", p.cnt["sc"]) for i in range(2)}
        w1g = {0: None, 1: None}
        w2g = {0: None, 1: None}

        def take(n=1):
            for _ in range(n):
                if not p5q:
                    break
                p5q.pop(0)()

        g2_done = None
        for grp in range(2):
            for ti in range(4):
                gt = blk * 8 + grp * 4 + ti
                slot = ti % 2
                ssl = "xs0" if slot == 0 else "xs1"

                @p.on("sync")
                def f(e, gt=gt, slot=slot, ssl=ssl, g=p.cnt["tv"]):
                    W(e, "tv", g)
                    e.dma_start(out=xmb[slot],
                                in_=scr_xmid[gt * 128:(gt + 1) * 128, :])\
                     .then_inc(SEM[ssl], 16)
                xtgt = p.inc(ssl, 16)

                @p.on("vector")
                def f(e, gt=gt, slot=slot, ssl=ssl, xtgt=xtgt):
                    W(e, ssl, xtgt)
                    e.tensor_scalar(xmb[slot], xmb[slot],
                                    ln2ab[:, 2 * gt:2 * gt + 1],
                                    ln2ab[:, 2 * gt + 1:2 * gt + 2],
                                    op0=ALU.add, op1=ALU.mult)
                    e.drain().then_inc(SEM["vv"], 1)
                vv_nm = p.inc("vv")

                @p.on("tensor")
                def f(e, slot=slot, vv_nm=vv_nm, g0=zg[0], g1=zg[1]):
                    W(e, "vv", vv_nm)
                    WT(e, g0, g1)
                    for kc in range(DC):
                        e.transpose(pZ[kc // 8][0:128, (kc % 8) * 128:(kc % 8) * 128 + 128],
                                    xmb[slot][:, kc * 128:(kc + 1) * 128], ident[:])
                    e.drain().then_inc(SEM["tv"], 1)
                tv_tr = p.inc("tv")

                @p.on("vector")
                def f(e, ti=ti, soff=soff, tv_tr=tv_tr):
                    W(e, "tv", tv_tr)
                    for kc in range(DC):
                        base = kc * 512 + ti * 128
                        e.tensor_scalar(h2[:, base:base + 128],
                                        pZ[kc // 8][0:128, (kc % 8) * 128:(kc % 8) * 128 + 128],
                                        modv[:, soff + 3 * DC + kc:soff + 3 * DC + kc + 1],
                                        modv[:, soff + 2 * DC + kc:soff + 2 * DC + kc + 1],
                                        op0=ALU.mult, op1=ALU.add)
                    e.drain().then_inc(SEM["vv"], 1)
                zg[0] = zg[1] = ("vv", p.inc("vv"))
                take(1)
            h2_done = p.cnt["vv"]

            @p.on("scalar")
            def f(e, h2_done=h2_done):
                W(e, "vv", h2_done)
                e.activation(g2, h2, AF.Gelu).then_inc(SEM["sc"], 1)
            g2_done = p.inc("sc")

            for lc in range(LCH):
                pw = lc % 2
                @p.on("gpsimd")
                def f(e, lc=lc, g=("tv", p.cnt["tv"])):
                    WT(e, g)
                    e.dma_start(out=dwn.rearrange("p (c f) -> p c f", f=128),
                                in_=down_w[:, lc * 128:(lc + 1) * 128]
                                .rearrange("(c p) f -> p c f", p=128))\
                     .then_inc(SEM["ws0"], 16)
                dw_t = p.inc("ws0", 16)

                @p.on("tensor")
                def f(e, lc=lc, pw=pw, dw_t=dw_t, h2_done=h2_done, fr=pg[pw]):
                    W(e, "ws0", dw_t)
                    if lc == 0:
                        W(e, "vv", h2_done)
                    WT(e, fr)
                    for kc in range(DC):
                        e.matmul(pW[pw][0:128, 0:512],
                                 dwn[:, kc * 128:(kc + 1) * 128],
                                 h2[:, kc * 512:(kc + 1) * 512],
                                 start=(kc == 0), stop=(kc == DC - 1))
                    e.drain().then_inc(SEM["tv"], 1)
                tv_d = p.inc("tv")

                @p.on("scalar")
                def f(e, lc=lc, grp=grp, pw=pw, tv_d=tv_d, gbf=GBF[blk % 2],
                      gbuf=gbuf):
                    W(e, "tv", max(tv_d, gbf))
                    e.activation(gbuf[:, lc * 1024 + grp * 512:lc * 1024 + grp * 512 + 512],
                                 pW[pw][0:128, 0:512], AF.Gelu,
                                 bias=dbfm[:, lc:lc + 1]).then_inc(SEM["sc"], 1)
                pg[pw] = ("sc", p.inc("sc"))
                take(1)

            @p.on("tensor")
            def f(e, h2_done=h2_done, fr=pg[2]):
                W(e, "vv", h2_done)
                WT(e, fr)
                for kc in range(DC):
                    e.matmul(pW[2][0:8, 0:512], rwsb[:, kc * 8:(kc + 1) * 8],
                             h2[:, kc * 512:(kc + 1) * 512],
                             start=(kc == 0), stop=(kc == DC - 1))
                e.drain().then_inc(SEM["tv"], 1)
            tv_r = p.inc("tv")

            @p.on("vector")
            def f(e, tv_r=tv_r):
                W(e, "tv", tv_r)
                e.tensor_copy(ltmp[:], pW[2][0:8, 0:512])
                e.drain().then_inc(SEM["vv"], 1)
            vv_lt = p.inc("vv")
            pg[2] = ("vv", vv_lt)

            @p.on("tensor")
            def f(e, vv_lt=vv_lt, fr=pg[3]):
                W(e, "vv", vv_lt)
                WT(e, fr)
                for t4 in range(4):
                    e.transpose(pW[3][0:128, t4 * 8:t4 * 8 + 8],
                                ltmp[:, t4 * 128:(t4 + 1) * 128], ident[0:8, 0:8])
                e.drain().then_inc(SEM["tv"], 1)
            tv_lt = p.inc("tv")

            @p.on("vector")
            def f(e, grp=grp, tv_lt=tv_lt, sx_rbB=sx_rbB):
                W(e, "tv", tv_lt)
                W(e, "sx", sx_rbB)
                for t4 in range(4):
                    col = (grp * 4 + t4) * 8
                    e.tensor_tensor(logit_s[:, col:col + 8],
                                    pW[3][0:128, t4 * 8:t4 * 8 + 8], rbadjB[:],
                                    op=ALU.add)
                e.drain().then_inc(SEM["vv"], 1)
            pg[3] = ("vv", p.inc("vv"))
            take(1)

            for dg in range(4):
                @p.on("gpsimd")
                def f(e, dg=dg, g=("tv", p.cnt["tv"])):
                    WT(e, g)
                    e.dma_start(out=cwb.rearrange("p (c f) -> p c f", c=DC),
                                in_=core_w[:, dg * 512:(dg + 1) * 512]
                                .rearrange("(c p) f -> p c f", p=128))\
                     .then_inc(SEM["w20"], 16)
                cw_t = p.inc("w20", 16)

                for tcl in range(4):
                    pw = tcl % 2
                    @p.on("tensor")
                    def f(e, tcl=tcl, pw=pw, cw_t=cw_t, g2d=g2_done, fr=pg[pw]):
                        if tcl == 0:
                            W(e, "w20", cw_t)
                            W(e, "sc", g2d)
                        WT(e, fr)
                        for kc in range(DC):
                            e.matmul(pW[pw][0:128, 0:512],
                                     g2[:, kc * 512 + tcl * 128:kc * 512 + tcl * 128 + 128],
                                     cwb[:, kc * 512:(kc + 1) * 512],
                                     start=(kc == 0), stop=(kc == DC - 1))
                        e.drain().then_inc(SEM["tv"], 1)
                    tv_c = p.inc("tv")

                    @p.on("vector")
                    def f(e, tv_c=tv_c, pw=pw):
                        W(e, "tv", tv_c)
                        e.tensor_copy(ctmp[:], pW[pw][0:128, 0:512])
                        e.drain().then_inc(SEM["vv"], 1)
                    vv_ct = p.inc("vv")
                    pg[pw] = ("vv", vv_ct)
                    take(1)

                    @p.on("sync")
                    def f(e, blk=blk, grp=grp, tcl=tcl, dg=dg, vv_ct=vv_ct):
                        W(e, "vv", vv_ct)
                        tok = blk * 1024 + grp * 512 + tcl * 128
                        e.dma_start(out=scr_core[tok:tok + 128, dg * 512:(dg + 1) * 512],
                                    in_=ctmp[:]).then_inc(SEM["qs0"], 16)
                    qst = p.inc("qs0", 16)

                    @p.on("vector")
                    def f(e, qst=qst):
                        W(e, "qs0", qst)
                        e.drain().then_inc(SEM["vv"], 1)
                    p.inc("vv")

        for t8 in range(8):
            @p.on("vector")
            def f(e, t8=t8):
                e.reduce_max(rm[:], logit_s[:, t8 * 8:(t8 + 1) * 8], axis=AX.X)
                e.drain()
                e.tensor_scalar(rm[:], rm[:], -1.0, None, op0=ALU.mult)
                e.drain().then_inc(SEM["vv"], 1)
            vv_m = p.inc("vv")

            @p.on("scalar")
            def f(e, t8=t8, vv_m=vv_m):
                W(e, "vv", vv_m)
                e.activation(rex[:], logit_s[:, t8 * 8:(t8 + 1) * 8], AF.Exp,
                             bias=rm[:], accum_out=rz_[:]).then_inc(SEM["sc"], 1)
            sc_e = p.inc("sc")

            @p.on("vector")
            def f(e, sc_e=sc_e):
                W(e, "sc", sc_e)
                e.reciprocal(rm[:], rz_[:])
                e.drain()
                e.tensor_scalar(rex[:], rex[:], rm[:], None, op0=ALU.mult)
                e.drain()
                e.max(srt[:], rex[:])
                e.drain()
                e.tensor_tensor(rg1[:], srt[:, 1:2], srt[:, 0:1], op=ALU.subtract)
                e.drain().then_inc(SEM["vv"], 1)
            vv_p = p.inc("vv")

            @p.on("scalar")
            def f(e, vv_p=vv_p):
                W(e, "vv", vv_p)
                e.activation(rg1[:], rg1[:], AF.Exp).then_inc(SEM["sc"], 1)
            sc_g = p.inc("sc")

            @p.on("vector")
            def f(e, t8=t8, sc_g=sc_g):
                W(e, "sc", sc_g)
                e.tensor_scalar(rg0[:], rg1[:], 1.0, None, op0=ALU.add)
                e.drain()
                e.reciprocal(rg0[:], rg0[:])
                e.drain()
                e.tensor_tensor(rg1[:], rg1[:], rg0[:], op=ALU.mult)
                e.drain()
                e.tensor_scalar(rt0[:], rex[:], srt[:, 0:1], rg0[:],
                                op0=ALU.is_equal, op1=ALU.mult)
                e.drain()
                e.tensor_scalar(w10s[:, 0:8], rex[:], srt[:, 1:2], rg1[:],
                                op0=ALU.is_equal, op1=ALU.mult)
                e.drain()
                e.tensor_tensor(w10s[:, 0:8], w10s[:, 0:8], rt0[:], op=ALU.add)
                e.tensor_tensor(w10s[:, 8:9], rg0[:], rg1[:], op=ALU.add)
                e.memset(w10s[:, 9:10], 1.0)
                e.drain().then_inc(SEM["vv"], 1)
            vv_w = p.inc("vv")

            @p.on("tensor")
            def f(e, vv_w=vv_w, fr=pg[3]):
                W(e, "vv", vv_w)
                WT(e, fr)
                e.transpose(pW[3][0:10, 0:128], w10s[:], ident[:])
                e.drain().then_inc(SEM["tv"], 1)
            tv_w10 = p.inc("tv")

            @p.on("vector")
            def f(e, t8=t8, tv_w10=tv_w10):
                W(e, "tv", tv_w10)
                e.tensor_copy(w10fm[:, t8 * 128:(t8 + 1) * 128], pW[3][0:10, 0:128])
                e.drain().then_inc(SEM["vv"], 1)
            pg[3] = ("vv", p.inc("vv"))
        w10_done = p.cnt["vv"]

        @p.on("gpsimd")
        def f(e, w10_done=w10_done):
            W(e, "vv", w10_done)
            for r in range(9):
                e.dma_start(out=scr_rows[6 + r:7 + r, 0:1024],
                            in_=w10fm[r:r + 1, :]).then_inc(SEM["sg"], 16)
        sx_w10 = p.inc("sg", 16 * 9)

        take(len(p5q))
        if blk in (0, 2):
            @p.on("sync")
            def f(e, s_=s_, g=p.cnt["vv"], gs=p.cnt["sx"]):
                W(e, "vv", g)
                W(e, "sx", gs)
                e.dma_start(out=bcastA[:],
                            in_=scr_rows[s_ * 2 + 1, 0:2048].unsqueeze(0)
                            .partition_broadcast(128)).then_inc(SEM["sx"], 16)
            SXGB[s_] = p.inc("sx", 16)
        w1g[0] = w1g[1] = ("tv", p.cnt["tv"])
        w2g[0] = w2g[1] = ("tv", p.cnt["tv"])
        for ei in range(9):
            shared = (ei == 8)
            h_e = L if shared else EHID[ei]
            hch = h_e // 128
            w1_src = shared_w1 if shared else ew1[ei]
            w2_src = shared_w2 if shared else ew2[ei]
            scale = 0.1 if shared else 1.0
            hid_free = ("tv", p.cnt["tv"])

            b1_t = None
            we_done = None
            if not shared:
                @p.on("gpsimd")
                def f(e, ei=ei, hch=hch, g=("sc", p.cnt["sc"]),
                      pcw=(PC_TOTAL if blk == 0 and ei == 0 else 0)):
                    WT(e, g)
                    if pcw:
                        W(e, "pc", pcw)
                    e.dma_start(out=b1f[:, 0:hch],
                                in_=eb1[ei].rearrange("(c p) -> p c", p=128),
                                allow_slow_non_contiguous=True).then_inc(SEM["sg"], 16)
                b1_t = p.inc("sg", 16)

                @p.on("gpsimd")
                def f(e, ei=ei, sx_w10=sx_w10, g=p.cnt["vv"], gs=p.cnt["sg"]):
                    W(e, "sg", max(sx_w10, gs))
                    W(e, "vv", g)
                    e.dma_start(out=webb[:], in_=scr_rows[6 + ei, 0:1024].unsqueeze(0)
                                .partition_broadcast(128)).then_inc(SEM["sg"], 16)
                we_done_sg = p.inc("sg", 16)
                we_done = None

            gb1_done = None
            if blk == 0 and not shared:
                @p.on("scalar")
                def f(e, hch=hch, b1_t=b1_t, gt=p.cnt["tv"]):
                    W(e, "sg", b1_t)
                    W(e, "tv", gt)
                    e.activation(gb1sb[:, 0:hch], b1f[:, 0:hch], AF.Gelu)\
                     .then_inc(SEM["sc"], 1)
                gb1_done = p.inc("sc")

            for hs in range(h_e // 512):
                sl1 = hs % 2
                ssem1 = "ws1" if sl1 == 0 else "w11"
                @p.on("gpsimd")
                def f(e, hs=hs, sl1=sl1, ssem1=ssem1, w1_src=w1_src,
                      g=w1g[sl1]):
                    WT(e, g)
                    e.dma_start(out=w1s[sl1].rearrange("p (c f) -> p c f", c=8),
                                in_=w1_src[:, hs * 512:(hs + 1) * 512]
                                .rearrange("(c p) f -> p c f", p=128))\
                     .then_inc(SEM[ssem1], 16)
                w1_t = p.inc(ssem1, 16)

                for hcl in range(4):
                    hc = hs * 4 + hcl
                    pz = hc % 2
                    @p.on("tensor")
                    def f(e, hcl=hcl, pz=pz, sl1=sl1, ssem1=ssem1, w1_t=w1_t,
                          fr=zg[pz], hf=hid_free, gg=("sc", p.cnt["sc"])):
                        if hcl == 0:
                            W(e, ssem1, w1_t)
                            WT(e, gg)
                        WT(e, fr, hf)
                        for half in range(2):
                            for kc in range(LCH):
                                e.matmul(pZ[pz][0:128, half * 512:half * 512 + 512],
                                         w1s[sl1][:, kc * 512 + hcl * 128:kc * 512 + hcl * 128 + 128],
                                         gbuf[:, kc * 1024 + half * 512:kc * 1024 + half * 512 + 512],
                                         start=(kc == 0), stop=(kc == LCH - 1))
                        e.drain().then_inc(SEM["tv"], 1)
                    tv_z = p.inc("tv")
                    if hcl == 3:
                        w1g[sl1] = ("tv", tv_z)

                    @p.on("scalar")
                    def f(e, hc=hc, pz=pz, tv_z=tv_z, shared=shared):
                        W(e, "tv", tv_z)
                        bias = (sb1fm[:, hc:hc + 1] if shared else b1f[:, hc:hc + 1])
                        e.activation(hid[:, hc * 1024:(hc + 1) * 1024],
                                     pZ[pz][0:128, 0:1024], AF.Gelu,
                                     bias=bias).then_inc(SEM["sc"], 1)
                    sc_h = p.inc("sc")
                    zg[pz] = ("sc", sc_h)

                    if not shared:
                        @p.on("vector")
                        def f(e, hc=hc, sc_h=sc_h, we_done=we_done_sg):
                            W(e, "sc", sc_h)
                            W(e, "sg", we_done)
                            e.tensor_tensor(hid[:, hc * 1024:(hc + 1) * 1024],
                                            hid[:, hc * 1024:(hc + 1) * 1024],
                                            webb[:], op=ALU.mult)
                            e.drain().then_inc(SEM["vv"], 1)
                        p.inc("vv")
            hid_done_v = p.cnt["vv"]
            hid_done_s = p.cnt["sc"]

            for lc in range(LCH):
                sl2 = lc % 2
                ssem2 = "w21" if sl2 == 0 else "w22"
                @p.on("gpsimd")
                def f(e, lc=lc, sl2=sl2, ssem2=ssem2, w2_src=w2_src, hch=hch,
                      g=w2g[sl2]):
                    WT(e, g)
                    e.dma_start(out=w2s[sl2][:, 0:hch * 128]
                                .rearrange("p (c f) -> p c f", f=128),
                                in_=w2_src[:, lc * 128:(lc + 1) * 128]
                                .rearrange("(c p) f -> p c f", p=128))\
                     .then_inc(SEM[ssem2], 16)
                w2_t = p.inc(ssem2, 16)

                for half in range(2):
                    pw = half % 2
                    @p.on("tensor")
                    def f(e, half=half, pw=pw, sl2=sl2, ssem2=ssem2, w2_t=w2_t,
                          hch=hch, fr=pg[pw], hv=hid_done_v, hsd=hid_done_s):
                        if half == 0:
                            W(e, ssem2, w2_t)
                            W(e, "vv", hv)
                            W(e, "sc", hsd)
                        WT(e, fr)
                        for hc in range(hch):
                            e.matmul(pW[pw][0:128, 0:512],
                                     w2s[sl2][:, hc * 128:(hc + 1) * 128],
                                     hid[:, hc * 1024 + half * 512:hc * 1024 + half * 512 + 512],
                                     start=(hc == 0), stop=(hc == hch - 1))
                        e.drain().then_inc(SEM["tv"], 1)
                    tv_w2 = p.inc("tv")
                    if half == 1:
                        w2g[sl2] = ("tv", tv_w2)

                    @p.on("vector")
                    def f(e, lc=lc, half=half, pw=pw, ei=ei, scale=scale, tv_w2=tv_w2):
                        W(e, "tv", tv_w2)
                        dst = moe[:, lc * 1024 + half * 512:lc * 1024 + half * 512 + 512]
                        if ei == 0:
                            e.tensor_copy(dst, pW[pw][0:128, 0:512])
                        else:
                            e.scalar_tensor_tensor(dst, pW[pw][0:128, 0:512], scale,
                                                   dst, op0=ALU.mult, op1=ALU.add)
                        e.drain().then_inc(SEM["vv"], 1)
                    pg[pw] = ("vv", p.inc("vv"))

                if blk == 0 and not shared:
                    @p.on("tensor")
                    def f(e, hch=hch, sl2=sl2, gb1_done=gb1_done, fr=pg[2]):
                        W(e, "sc", gb1_done)
                        WT(e, fr)
                        for hc in range(hch):
                            e.matmul(pW[2][0:128, 0:1],
                                     w2s[sl2][:, hc * 128:(hc + 1) * 128],
                                     gb1sb[:, hc:hc + 1],
                                     start=(hc == 0), stop=(hc == hch - 1))
                        e.drain().then_inc(SEM["tv"], 1)
                    tv_cp = p.inc("tv")
                    w2g[sl2] = ("tv", tv_cp)

                    @p.on("vector")
                    def f(e, tv_cp=tv_cp):
                        W(e, "tv", tv_cp)
                        e.tensor_copy(ctmp[:, 0:1], pW[2][0:128, 0:1])
                        e.drain().then_inc(SEM["vv"], 1)
                    vv_cp = p.inc("vv")
                    pg[2] = ("vv", vv_cp)

                    @p.on("sync")
                    def f(e, ei=ei, lc=lc, vv_cp=vv_cp):
                        W(e, "vv", vv_cp)
                        e.dma_start(out=scr_c10[ei:ei + 1, lc * 128:(lc + 1) * 128],
                                    in_=ctmp[:, 0:1]).then_inc(SEM["qs0"], 16)
                    qcp = p.inc("qs0", 16)

                    @p.on("vector")
                    def f(e, qcp=qcp):
                        W(e, "qs0", qcp)
                        e.drain().then_inc(SEM["vv"], 1)
                    p.inc("vv")

        if blk == 0:
            @p.on("sync")
            def f(e, g=p.cnt["qs0"], gt=p.cnt["tv"], gs=p.cnt["sx"]):
                W(e, "qs0", g)
                W(e, "tv", gt)
                W(e, "sx", gs)
                e.dma_start(out=ct8, in_=scr_c10[0:8, :]).then_inc(SEM["sx"], 16)
            sx_c8 = p.inc("sx", 16)

            @p.on("vector")
            def f(e, sx_c8=sx_c8):
                W(e, "sx", sx_c8)
                e.tensor_scalar(c10sb[0:8, :], ct8, -1.0, None, op0=ALU.mult)
                e.drain()
                e.tensor_tensor(ct8, ct8, b2s[:], op=ALU.add)
                e.drain().then_inc(SEM["vv"], 1)
            vv_c8 = p.inc("vv")

            @p.on("tensor")
            def f(e, vv_c8=vv_c8, f2=pg[2], f3=pg[3]):
                W(e, "vv", vv_c8)
                WT(e, f2, f3)
                e.matmul(pW[3][0:1, 0:512], one8[:], ct8[:, 0:512], start=True, stop=True)
                e.matmul(pW[2][0:1, 0:512], one8[:], ct8[:, 512:1024], start=True, stop=True)
                e.drain().then_inc(SEM["tv"], 1)
            tv_ct = p.inc("tv")

            @p.on("vector")
            def f(e, tv_ct=tv_ct):
                W(e, "tv", tv_ct)
                e.tensor_copy(ctrow[:, 0:512], pW[3][0:1, 0:512])
                e.tensor_copy(ctrow[:, 512:1024], pW[2][0:1, 0:512])
                e.drain().then_inc(SEM["vv"], 1)
            vv_ctr = p.inc("vv")
            pg[2] = pg[3] = ("vv", vv_ctr)

            @p.on("gpsimd")
            def f(e, vv_ctr=vv_ctr):
                W(e, "vv", vv_ctr)
                e.dma_start(out=c10sb[8:9, :], in_=ctrow).then_inc(SEM["sg"], 16)
            sg_c = p.inc("sg", 16)

            @p.on("vector")
            def f(e, sg_c=sg_c):
                W(e, "sg", sg_c)
                e.tensor_scalar(ctrow, sb2r[:], 0.1, None, op0=ALU.mult)
                e.drain().then_inc(SEM["vv"], 1)
            vv_s2 = p.inc("vv")

            @p.on("gpsimd")
            def f(e, vv_s2=vv_s2):
                W(e, "vv", vv_s2)
                e.dma_start(out=c10sb[9:10, :], in_=ctrow).then_inc(SEM["sg"], 16)
            sg_c2 = p.inc("sg", 16)
            c10_ready = ("sg", sg_c2)

        for lc in range(LCH):
            for hf in range(2):
                pw = (lc * 2 + hf) % 2
                @p.on("tensor")
                def f(e, lc=lc, hf=hf, pw=pw, fr=pg[pw], cg=c10_ready,
                      wg=("vv", w10_done)):
                    WT(e, fr, cg, wg)
                    e.matmul(pW[pw][0:128, 0:512],
                             c10sb[0:10, lc * 128:(lc + 1) * 128],
                             w10fm[0:10, hf * 512:(hf + 1) * 512],
                             start=True, stop=True)
                    e.drain().then_inc(SEM["tv"], 1)
                tv_cx = p.inc("tv")

                @p.on("vector")
                def f(e, lc=lc, hf=hf, pw=pw, tv_cx=tv_cx):
                    W(e, "tv", tv_cx)
                    a = lc * 1024 + hf * 512
                    e.scalar_tensor_tensor(moebf[:, a:a + 512],
                                           pW[pw][0:128, 0:512], 1.0,
                                           moe[:, a:a + 512],
                                           op0=ALU.mult, op1=ALU.add)
                    e.drain().then_inc(SEM["vv"], 1)
                pg[pw] = ("vv", p.inc("vv"))

        moebf_done = p.cnt["vv"]
        steps = []
        upt_box = {}

        for dg in range(4):
            us = dg % 2

            def s_dma(dg=dg, us=us):
                @p.on("gpsimd")
                def f(e, dg=dg, us=us, pin=p.cnt["up0"],
                      g=("tv", p5upt[us] if dg >= 2 else p.cnt["tv"])):
                    WT(e, g)
                    W(e, "up0", pin)
                    e.dma_start(out=ups2[us].rearrange("p (c f) -> p c f", c=LCH),
                                in_=up_w[:, dg * 512:(dg + 1) * 512]
                                .rearrange("(c p) f -> p c f", p=128))\
                     .then_inc(SEM["up0"], 16)
                upt_box[dg] = p.inc("up0", 16)
            steps.append(s_dma)

            for tc in range(8):
                def s_tc(dg=dg, us=us, tc=tc, blk=blk, s_=s_, moebf=moebf,
                         moebf_done=moebf_done):
                    tok = blk * 1024 + tc * 128
                    sl = tc % 2
                    xsl = "cs0" if sl == 0 else "cs1"

                    @p.on("sync")
                    def f(e, tok=tok, sl=sl, dg=dg, xsl=xsl, g=p5xf[sl]):
                        if g:
                            W(e, "vv", g)
                        e.dma_start(out=xms[sl],
                                    in_=scr_xmid[tok:tok + 128,
                                                 dg * 512:(dg + 1) * 512])\
                         .then_inc(SEM[xsl], 16)
                        e.dma_start(out=cbs[sl],
                                    in_=scr_core[tok:tok + 128,
                                                 dg * 512:(dg + 1) * 512])\
                         .then_inc(SEM[xsl], 16)
                    xtg = p.inc(xsl, 32)

                    pw = tc % 2
                    @p.on("tensor")
                    def f(e, tc=tc, dg=dg, us=us, pw=pw, fr=pg[pw],
                          md=moebf_done, up_t=upt_box[dg], moebf=moebf):
                        if tc == 0:
                            W(e, "up0", up_t)
                            W(e, "vv", md)
                        WT(e, fr)
                        for lc in range(LCH):
                            e.matmul(pW[pw][0:128, 0:512],
                                     moebf[:, lc * 1024 + tc * 128:lc * 1024 + tc * 128 + 128],
                                     ups2[us][:, lc * 512:(lc + 1) * 512],
                                     start=(lc == 0), stop=False)
                        e.matmul(pW[pw][0:128, 0:512], onesb[0:1, 0:128],
                                 ubcbr[0:1, dg * 512:(dg + 1) * 512],
                                 start=False, stop=True)
                        if tc == 7:
                            e._e.drain().then_inc(SEM["tv"], 1)
                        else:
                            e.drain().then_inc(SEM["tv"], 1)
                    tv_u = p.inc("tv")
                    if tc == 7:
                        p5upt[us] = tv_u
                        if dg == 3:
                            GBF[blk % 2] = tv_u

                    @p.on("vector")
                    def f(e, sl=sl, dg=dg, pw=pw, tv_u=tv_u, xsl=xsl, xtg=xtg,
                          sx_gb=SXGB[s_], fso=p5fso[sl],
                          ssem="so" if sl == 0 else "so2"):
                        W(e, "tv", tv_u)
                        W(e, xsl, xtg)
                        W(e, "sx", sx_gb)
                        if fso:
                            W(e, ssem, fso)
                        e.tensor_tensor(fins[sl], pW[pw][0:128, 0:512],
                                        cbs[sl], op=ALU.add)
                        e.drain()
                        e.tensor_tensor(fins[sl], fins[sl],
                                        bcastA[:, dg * 512:(dg + 1) * 512],
                                        op=ALU.mult)
                        e.drain()
                        e.tensor_tensor(fins[sl], xms[sl], fins[sl], op=ALU.add)
                        e.drain().then_inc(SEM["vv"], 1)
                    vv_f = p.inc("vv")
                    pg[pw] = ("vv", vv_f)
                    p5xf[sl] = vv_f

                    @p.on("sync")
                    def f(e, s_=s_, tok=tok, dg=dg, sl=sl, vv_f=vv_f):
                        W(e, "vv", vv_f)
                        e.dma_start(out=out[s_, tok - s_ * S:tok - s_ * S + 128,
                                            dg * 512:(dg + 1) * 512],
                                    in_=fins[sl])\
                         .then_inc(SEM["so" if sl == 0 else "so2"], 16)
                    p5fso[sl] = p.inc("so" if sl == 0 else "so2", 16)
                steps.append(s_tc)

        return c10_ready, steps

    c10_ready = None
    p5q = []
    for blk in range(4):
        c10_ready, p5q = emit_blk(blk, c10_ready, p5q)
    for f_ in p5q:
        f_()

    @p.on("sync")
    def f(e, w=p.cnt["so"], w2=p.cnt["so2"]):
        if w:
            e.wait_ge(SEM["so"], w)
        if w2:
            e.wait_ge(SEM["so2"], w2)

    st.close()
    pp.close()


def build(debug=(), phases=("p0", "p1", "p2", "p345")):
    dbg = set(debug)
    PH = set(phases)
    nc = bass.Bass("TRN2", target_bir_lowering=False)
    ctx = ExitStack()
    p = P(nc, ctx)
    SEMN = ("sx", "sxa", "sg", "tv", "vv", "sc", "gc", "so", "so2", "pc", "qs0",
            "xs0", "xs1", "cs0", "cs1", "ws0", "ws1", "w20", "w21", "w11", "w22", "up0")
    for s_ in SEMN:
        p.sem(s_)
    SEM = p.sems

    LATE = {}
    def W(e, semname, val):
        if callable(val):
            val = val()
        if val > 0:
            e.wait_ge(SEM[semname], val)

    def WG(e, buf):
        sn, v = p.guard(buf)
        if sn is not None and v > 0:
            e.wait_ge(SEM[sn], v)

    def barrier():
        tgt = {k: p.cnt[k] for k in SEMN}
        for eng in P.ENGS:
            @p.on(eng)
            def f(e, tgt=dict(tgt)):
                for k, v in tgt.items():
                    if v:
                        e.wait_ge(SEM[k], v)

    din = {}
    def I(name, shape):
        din[name] = nc.dram_tensor(name, list(shape), F32, kind="ExternalInput").ap()
        return din[name]

    x_in = I("x", (NSEQ, S, D)); c_in = I("c", (NSEQ, D))
    wqkv = I("Wqkv", (D, QKV)); bqkv = I("bqkv", (QKV,))
    wout_w = I("Wout", (D, D)); bout = I("bout", (D,))
    down_w = I("down_w", (D, L)); down_b = I("down_b", (L,))
    up_w = I("up_w", (L, D)); up_b = I("up_b", (D,))
    router_w = I("router_w", (D, 8)); router_b = I("router_b", (8,))
    shared_w1 = I("shared_w1", (L, L)); shared_b1 = I("shared_b1", (L,))
    shared_w2 = I("shared_w2", (L, L)); shared_b2 = I("shared_b2", (L,))
    core_w = I("core_w", (D, D)); core_b = I("core_b", (D,))
    e_w1 = [I(f"e{e}_w1", (L, EHID[e])) for e in range(8)]
    e_b1 = [I(f"e{e}_b1", (EHID[e],)) for e in range(8)]
    e_w2 = [I(f"e{e}_w2", (EHID[e], L)) for e in range(8)]
    e_b2 = [I(f"e{e}_b2", (L,)) for e in range(8)]
    wmod = I("wmod", (D, 6 * D))
    ident_in = I("ident_f32", (128, 128))
    perm_in = I("rope_perm", (128, 128))
    masks_in = I("causal_masks", (128, 2048))
    nsin_in = I("n_sin", (16, S)); ncos_in = I("n_cos", (16, S))
    rcost_in = I("router_cost", (1, 8))
    selc_in = I("rope_selc", (16, 128))
    sel1_in = I("rope_sel1", (1, 128))
    sels_in = I("rope_sels", (16, 128))

    out = nc.dram_tensor("out", [NSEQ, S, D], F32, kind="ExternalOutput").ap()
    dbg_t = {}
    def DBG(name, shape, dtype=F32):
        if name in dbg:
            dbg_t[name] = nc.dram_tensor("dbg_" + name, list(shape), dtype,
                                         kind="ExternalOutput").ap()
        return dbg_t.get(name)

    scr_q = nc.dram_tensor("scr_q", [64, NH * S], BF16, kind="Internal").ap()
    scr_xmid = nc.dram_tensor("scr_xmid", [T, D], F32, kind="Internal").ap()
    scr_core = nc.dram_tensor("scr_core", [T, D], F32, kind="Internal").ap()
    scr_rows = nc.dram_tensor("scr_rows", [16, 4096], F32, kind="Internal").ap()
    scr_c10 = nc.dram_tensor("scr_c10", [16, L], F32, kind="Internal").ap()

    scr_e1b = [nc.dram_tensor(f"scr_e1b{e}", [L, EHID[e]], BF16,
                              kind="Internal").ap() for e in range(8)]
    scr_e2b = [nc.dram_tensor(f"scr_e2b{e}", [EHID[e], L], BF16,
                              kind="Internal").ap() for e in range(8)]
    scr_s1b = nc.dram_tensor("scr_s1b", [L, L], BF16, kind="Internal").ap()
    scr_s2b = nc.dram_tensor("scr_s2b", [L, L], BF16, kind="Internal").ap()

    PRECAST = []

    def _pc(dst, src, n):
        for c in range(4):
            a, b = c * n // 4, (c + 1) * n // 4
            def f(e, dst=dst[a:b, :], src=src[a:b, :]):
                e.dma_start(out=dst, in_=src).then_inc(SEM["pc"], 16)
            PRECAST.append(f)

    for e_ in range(8):
        _pc(scr_e1b[e_], e_w1[e_], L)
        _pc(scr_e2b[e_], e_w2[e_], EHID[e_])
    _pc(scr_s1b, shared_w1, L)
    _pc(scr_s2b, shared_w2, L)
    PC_TOTAL = 16 * len(PRECAST)

    def emit_pc(n):
        for _ in range(min(n, len(PRECAST))):
            p.q["gpsimd"].append(PRECAST.pop(0))
            p.inc("pc", 16)

    sbp = ExitStack()
    def SB(name, shape, dtype, st=sbp):
        return st.enter_context(nc.sbuf_tensor(name, list(shape), dtype))

    ident = SB("ident", (128, 128), F32)
    onesb = SB("onesb", (1, 512), BF16)
    cpool = SB("cpool", (128, 4), F32)
    modv = SB("modv", (128, NSEQ * 4 * DC), F32)
    bqkv_fm = SB("bqkv_fm", (128, 16), F32)
    bkfm = SB("bkfm", (64, 8), F32)
    dbfm = SB("dbfm", (128, LCH), F32)
    sb1fm = SB("sb1fm", (128, LCH), F32)
    ln2ab = SB("ln2ab", (128, 2 * (T // 128)), F32)

    ubcbr = SB("ubcbr", (1, D), BF16)
    bcastA = SB("bcastA", (128, 2048), F32)
    rbadj = SB("rbadj", (1, 8), F32)
    rwsb = SB("rwsb", (128, DC * 8), BF16)
    gb1sb = SB("gb1sb", (128, 32), BF16)
    b2s = SB("b2s", (8, L), F32)
    sb2r = SB("sb2r", (1, L), F32)
    sb12 = ExitStack()
    masksb = SB("masksb", (128, 2048), BF16, st=sb12)
    ropeC = SB("ropeC", (128, S), BF16, st=sb12)
    ropeS = SB("ropeS", (128, S), BF16, st=sb12)
    identb = SB("identb", (128, 128), BF16, st=sb12)
    permb = SB("permb", (128, 128), BF16, st=sb12)
    onesf = SB("onesf", (1, 512), F32, st=sb12)
    bvB = SB("bvB", (128, 512), F32, st=sb12)
    boutr = SB("boutr", (1, D), BF16, st=sb12)

    nc.const_aps.register(F32, EPS, cpool[:, 0:1])

    @p.on("sync")
    def f(e):
        e.dma_start(out=ident[:], in_=ident_in).then_inc(SEM["sxa"], 16)
        e.dma_start(out=bvB[:], in_=bqkv[2560:3072].unsqueeze(0)
                    .partition_broadcast(128)).then_inc(SEM["sxa"], 16)
        e.dma_start(out=rbadj[:], in_=router_b.unsqueeze(0)).then_inc(SEM["sxa"], 16)
        e.dma_start(out=bqkv_fm[:], in_=bqkv[0:2048].rearrange("(c p) -> p c", p=128),
                    allow_slow_non_contiguous=True).then_inc(SEM["sxa"], 16)
        e.dma_start(out=bkfm[:], in_=bqkv[2048:2560].rearrange("(h k) -> k h", k=64),
                    allow_slow_non_contiguous=True).then_inc(SEM["sxa"], 16)
        e.dma_start(out=dbfm[:], in_=down_b.rearrange("(c p) -> p c", p=128),
                    allow_slow_non_contiguous=True).then_inc(SEM["sxa"], 16)
        e.dma_start(out=sb1fm[:], in_=shared_b1.rearrange("(c p) -> p c", p=128),
                    allow_slow_non_contiguous=True).then_inc(SEM["sxa"], 16)
        for ee in range(8):
            e.dma_start(out=b2s[ee:ee + 1, :], in_=e_b2[ee].unsqueeze(0)).then_inc(SEM["sxa"], 16)
        e.dma_start(out=sb2r[:], in_=shared_b2.unsqueeze(0)).then_inc(SEM["sxa"], 16)
        e.dma_start(out=selc_s[:], in_=selc_in).then_inc(SEM["sxa"], 16)
        e.dma_start(out=sel1_s[:], in_=sel1_in).then_inc(SEM["sxa"], 16)
        e.dma_start(out=sels_s[:], in_=sels_in).then_inc(SEM["sxa"], 16)
    sx_base_unused = p.inc("sxa", 16 * 19)

    @p.on("gpsimd")
    def f(e):
        e.dma_start(out=identb[:], in_=ident_in).then_inc(SEM["sg"], 16)
        e.dma_start(out=permb[:], in_=perm_in).then_inc(SEM["sg"], 16)
        e.dma_start(out=boutr[:], in_=bout.unsqueeze(0)).then_inc(SEM["sg"], 16)
        e.dma_start(out=masksb[:], in_=masks_in).then_inc(SEM["sg"], 16)
        e.dma_start(out=rwsb[:].rearrange("p (c e) -> p c e", e=8),
                    in_=router_w.rearrange("(c p) e -> p c e", p=128),
                    allow_slow_non_contiguous=True).then_inc(SEM["sg"], 16)
    sg_base = p.inc("sg", 80)

    p0 = ExitStack()
    selc_s = p0.enter_context(nc.sbuf_tensor("selc_s", [16, 128], F32))
    sel1_s = p0.enter_context(nc.sbuf_tensor("sel1_s", [1, 128], F32))
    sels_s = p0.enter_context(nc.sbuf_tensor("sels_s", [16, 128], F32))
    p0a = p0.enter_context(nc.sbuf_tensor("p0a", [16, S], F32))
    p0b = p0.enter_context(nc.sbuf_tensor("p0b", [16, S], F32))
    p0n = p0.enter_context(nc.sbuf_tensor("p0n", [16, S], F32))
    p0f = p0.enter_context(nc.sbuf_tensor("p0f", [16, 1], F32))
    p0sin = p0.enter_context(nc.sbuf_tensor("p0sin", [16, S], F32))
    ubtmp = p0.enter_context(nc.sbuf_tensor("ubtmp", [1, 2 * D], F32))
    rct = p0.enter_context(nc.sbuf_tensor("rct", [1, 8], F32))
    ctm = p0.enter_context(nc.sbuf_tensor("ctm", [NSEQ, D], F32))
    scfm = p0.enter_context(nc.sbuf_tensor("scfm", [128, 2 * DC], BF16))
    scfm32 = p0.enter_context(nc.sbuf_tensor("scfm32", [128, 2 * DC], F32))
    wt = [p0.enter_context(nc.sbuf_tensor(f"wt{i}", [128, DC * 512], BF16))
          for i in range(2)]
    wtF = [p0.enter_context(nc.sbuf_tensor(f"wtF{i}", [128, DC * 512], F32))
           for i in range(2)]
    mtmp = [p0.enter_context(nc.sbuf_tensor(f"mtmp{i}", [2, 512], F32))
            for i in range(2)]
    pmod2 = [p0.enter_context(nc.psum_tensor(f"pmod2{i}", [2, 512], F32))
             for i in range(2)]
    modsc = p0.enter_context(nc.sbuf_tensor("modsc", [128, 8], F32))
    modg = p0.enter_context(nc.sbuf_tensor("modg", [128, 8], F32))
    pmod = p0.enter_context(nc.psum_tensor("pmod", [128, 512], F32))
    ptr0 = p0.enter_context(nc.psum_tensor("ptr0", [128, 512], F32))

    @p.on("vector")
    def f(e):
        e.memset(cpool[:, 0:1], EPS)
        e.memset(onesb[:], 1.0)
        e.memset(onesf[:], 1.0)
        e.drain()
        e.drain().then_inc(SEM["vv"], 1)
    p.inc("vv")

    @p.on("gpsimd")
    def f(e):
        e.iota(p0a[:], pattern=[[1, S]], base=0, channel_multiplier=0,
               allow_small_or_imprecise_dtypes=True)
        e.iota(p0f[:], pattern=[[0, 1]], base=0, channel_multiplier=1,
               allow_small_or_imprecise_dtypes=True).then_inc(SEM["gc"], 1)
    gc_iota = p.inc("gc")

    @p.on("scalar")
    def f(e):
        W(e, "gc", gc_iota)
        e.activation(p0f[:], p0f[:], AF.Exp, scale=NEG_LN_BASE_16).then_inc(SEM["sc"], 1)
    sc_if = p.inc("sc")

    @p.on("sync")
    def f(e):
        e.dma_start(out=p0n[:], in_=nsin_in).then_inc(SEM["sxa"], 16)
        e.dma_start(out=ubtmp[:, 0:D], in_=up_b.unsqueeze(0)).then_inc(SEM["sxa"], 16)
        e.dma_start(out=ubtmp[:, D:2 * D], in_=core_b.unsqueeze(0)).then_inc(SEM["sxa"], 16)
        e.dma_start(out=rct[:], in_=rcost_in).then_inc(SEM["sxa"], 16)
        e.dma_start(out=ctm[:], in_=c_in).then_inc(SEM["sxa"], 16)
    sx_p0_unused = p.inc("sxa", 16 * 5)

    @p.on("vector")
    def f(e):
        W(e, "sc", sc_if)
        W(e, "sxa", lambda: LATE["sxa"])
        e.tensor_scalar(p0b[:], p0a[:], p0f[:], None, op0=ALU.mult)
        e.drain()
        e.scalar_tensor_tensor(p0sin[:], p0n[:], -TWO_PI, p0b[:], op0=ALU.mult, op1=ALU.add)
        e.drain()
        e.tensor_scalar(p0sin[:], p0sin[:], PI, -PI, op0=ALU.min, op1=ALU.max)
        e.drain()
        e.tensor_tensor(ubcbr[:], ubtmp[:, 0:D], ubtmp[:, D:2 * D], op=ALU.add)
        e.scalar_tensor_tensor(rbadj[:], rct[:], -COST_LAMBDA, rbadj[:],
                               op0=ALU.mult, op1=ALU.add)
        e.tensor_scalar(bkfm[:], bkfm[:], 0.125, None, op0=ALU.mult)
        e.drain()
        e.drain().then_inc(SEM["vv"], 1)
    vv_rr = p.inc("vv")

    @p.on("scalar")
    def f(e):
        W(e, "vv", vv_rr)
        e.activation(p0sin[:], p0sin[:], AF.Sin).then_inc(SEM["sc"], 1)
        e.activation(ctm[:], ctm[:], AF.Silu).then_inc(SEM["sc"], 1)
    sc_sin = p.inc("sc")
    sc_silu = p.inc("sc")

    LATE["sxa"] = p.cnt["sxa"]
    @p.on("sync")
    def f(e, w=vv_rr):
        W(e, "vv", w)
        e.dma_start(out=p0n[:], in_=ncos_in).then_inc(SEM["sxa"], 16)
    sx_nc = p.inc("sxa", 16)

    @p.on("vector")
    def f(e):
        W(e, "sxa", sx_nc)
        W(e, "sc", sc_sin)
        e.scalar_tensor_tensor(p0a[:], p0n[:], -TWO_PI, p0b[:], op0=ALU.mult, op1=ALU.add)
        e.drain()
        e.tensor_scalar(p0a[:], p0a[:], PI / 2, None, op0=ALU.add)
        e.drain()
        e.tensor_scalar(p0a[:], p0a[:], PI, -PI, op0=ALU.min, op1=ALU.max)
        e.drain()
        e.drain().then_inc(SEM["vv"], 1)
    vv_rc = p.inc("vv")

    @p.on("scalar")
    def f(e):
        W(e, "vv", vv_rc)
        e.activation(p0a[:], p0a[:], AF.Sin).then_inc(SEM["sc"], 1)
    sc_cos = p.inc("sc")

    for isC in (False, True):
        for ch in range(4):
            @p.on("tensor")
            def f(e, isC=isC, ch=ch, wv=vv_rc, ws=sc_cos, g=p.cnt["vv"]):
                W(e, "sc", ws)
                W(e, "vv", g)
                if isC:
                    e.matmul(ptr0[0:128, 0:512], selc_s[:, 0:128],
                             p0a[:, ch * 512:(ch + 1) * 512], start=True, stop=False)
                    e.matmul(ptr0[0:128, 0:512], sel1_s[:, 0:128],
                             onesf[0:1, 0:512], start=False, stop=True)
                else:
                    e.matmul(ptr0[0:128, 0:512], sels_s[:, 0:128],
                             p0sin[:, ch * 512:(ch + 1) * 512], start=True, stop=True)
                e.drain().then_inc(SEM["tv"], 1)
            tv_m = p.inc("tv")

            @p.on("vector")
            def f(e, isC=isC, ch=ch, tv_m=tv_m):
                W(e, "tv", tv_m)
                dst = ropeC if isC else ropeS
                e.tensor_copy(dst[:, ch * 512:(ch + 1) * 512], ptr0[0:128, 0:512])
                e.drain()
                e.drain().then_inc(SEM["vv"], 1)
            p.inc("vv")

    @p.on("tensor")
    def f(e, gv=p.cnt["vv"]):
        W(e, "sc", sc_silu)
        W(e, "sxa", lambda: LATE["sxa"])
        W(e, "vv", gv)
        for kc in range(DC):
            e.transpose(ptr0[0:128, kc * 2:kc * 2 + 2],
                        ctm[:, kc * 128:(kc + 1) * 128], ident[0:2, 0:2])
        e.drain().then_inc(SEM["tv"], 1)
    tv_ctr = p.inc("tv")

    @p.on("vector")
    def f(e):
        W(e, "tv", tv_ctr)
        e.tensor_copy(scfm[:], ptr0[0:128, 0:2 * DC])
        e.tensor_copy(scfm32[:], ptr0[0:128, 0:2 * DC])
        e.drain().then_inc(SEM["vv"], 1)
    vv_scfm = p.inc("vv")
    p.free_at("pmod", "vv", vv_scfm)

    last_mm = {}
    last_cp = {}
    mt_free = {}
    ptr_free = {}
    modg_sx = 0
    for t in range(24):
        q = t // 4
        hi = q in (2, 5)
        sl = t % 2
        buf = wtF[sl] if hi else wt[sl]
        ssem = (("cs0" if sl == 0 else "cs1") if hi
                else ("ws0" if sl == 0 else "ws1"))
        eng = "sync" if hi else "gpsimd"

        @p.on(eng)
        def f(e, t=t, buf=buf, ssem=ssem, g=last_mm.get((hi, sl), 0)):
            W(e, "tv", g)
            e.dma_start(out=buf[:].rearrange("p (c f) -> p c f", f=512),
                        in_=wmod[:, t * 512:(t + 1) * 512]
                        .rearrange("(c p) f -> p c f", p=128)).then_inc(SEM[ssem], 16)
        wtgt = p.inc(ssem, 16)
        if not hi:
            emit_pc(1)

        @p.on("tensor")
        def f(e, sl=sl, hi=hi, ssem=ssem, wtgt=wtgt, buf=buf, vtgt=vv_scfm,
              fr=last_cp.get(sl, 0)):
            W(e, ssem, wtgt)
            W(e, "vv", max(vtgt, fr))
            src = scfm32 if hi else scfm
            for kc in range(DC):
                e.matmul(pmod2[sl][0:2, 0:512], src[:, kc * 2:kc * 2 + 2],
                         buf[:, kc * 512:(kc + 1) * 512],
                         start=(kc == 0), stop=(kc == DC - 1))
            e.drain().then_inc(SEM["tv"], 1)
        last_mm[(hi, sl)] = p.inc("tv")

        @p.on("vector")
        def f(e, sl=sl, g=last_mm[(hi, sl)], mf=mt_free.get(sl, 0)):
            W(e, "tv", max(g, mf))
            e.tensor_copy(mtmp[sl][:], pmod2[sl][0:2, 0:512])
            e.drain().then_inc(SEM["vv"], 1)
        last_cp[sl] = p.inc("vv")

        @p.on("tensor")
        def f(e, sl=sl, cp_=last_cp[sl], pf=ptr_free.get(sl, 0)):
            W(e, "vv", max(cp_, pf))
            for c4 in range(4):
                e.transpose(pmod[0:128, sl * 8 + c4 * 2:sl * 8 + c4 * 2 + 2],
                            mtmp[sl][:, c4 * 128:(c4 + 1) * 128], ident[0:2, 0:2])
            e.drain().then_inc(SEM["tv"], 1)
        mt_free[sl] = tv_tr = p.inc("tv")

        @p.on("vector")
        def f(e, t=t, q=q, sl=sl, hi=hi, tvw=tv_tr, sxw=modg_sx):
            W(e, "tv", tvw)
            if hi:
                W(e, "sx", sxw)
                e.tensor_copy(modg[:], pmod[0:128, sl * 8:sl * 8 + 8])
                e.drain()
                e.tensor_scalar(modsc[:], modg[:], 0.0, None, op0=ALU.is_ge)
                e.drain()
                e.tensor_scalar(modsc[:], modsc[:], 2.0, -1.0, op0=ALU.mult, op1=ALU.add)
                e.drain()
                e.tensor_tensor(modg[:], modg[:], modsc[:], op=ALU.mult)
                e.drain()
                e.tensor_scalar(modg[:], modg[:], 0.25, None, op0=ALU.max)
                e.drain()
                e.tensor_tensor(modg[:], modg[:], modsc[:], op=ALU.mult)
                e.drain()
            else:
                qi = {0: 0, 1: 1, 3: 2, 4: 3}[q]
                for c4 in range(4):
                    chunk = (t * 4 + c4) % DC
                    co = sl * 8 + c4 * 2
                    for s_ in range(NSEQ):
                        col = s_ * 4 * DC + qi * DC + chunk
                        if qi in (1, 3):
                            e.tensor_scalar(modv[:, col:col + 1],
                                            pmod[0:128, co + s_:co + s_ + 1],
                                            1.0, None, op0=ALU.add)
                        else:
                            e.tensor_copy(modv[:, col:col + 1],
                                          pmod[0:128, co + s_:co + s_ + 1])
                e.drain()
            e.drain().then_inc(SEM["vv"], 1)
        ptr_free[sl] = last_ev = p.inc("vv")

        if hi:
            @p.on("sync")
            def f(e, t=t, q=q, vvw=last_ev):
                W(e, "vv", vvw)
                for c4 in range(4):
                    chunk = (t * 4 + c4) % DC
                    for s_ in range(NSEQ):
                        row = s_ * 2 + (0 if q == 2 else 1)
                        e.dma_start(out=scr_rows[row:row + 1,
                                                 chunk * 128:(chunk + 1) * 128],
                                    in_=modg[:, c4 * 2 + s_:c4 * 2 + s_ + 1])\
                         .then_inc(SEM["sx"], 16)
            modg_sx = p.inc("sx", 16 * 8)

    barrier()
    d = DBG("modv", (128, NSEQ * 4 * DC))
    if d is not None:
        @p.on("sync")
        def f(e, d=d):
            e.dma_start(out=d, in_=modv[:]).then_inc(SEM["so"], 16)
        p.inc("so", 16)
    d = DBG("rows", (4, 2048))
    if d is not None:
        @p.on("sync")
        def f(e, d=d):
            e.dma_start(out=d, in_=scr_rows[0:4, 0:2048]).then_inc(SEM["so"], 16)
        p.inc("so", 16)
    d = DBG("rope", (128, 2 * S))
    if d is not None:
        rpt = p0.enter_context(nc.sbuf_tensor("rpt", [128, 2 * S], F32))
        @p.on("vector")
        def f(e, w=p.cnt["vv"] + 1):
            e.tensor_copy(rpt[:, 0:S], ropeC[:])
            e.tensor_copy(rpt[:, S:2 * S], ropeS[:])
            e.drain().then_inc(SEM["vv"], 1)
        p.inc("vv")
        @p.on("sync")
        def f(e, d=d, rpt=rpt, w=p.cnt["vv"]):
            W(e, "vv", w)
            e.dma_start(out=d, in_=rpt[:]).then_inc(SEM["so"], 16)
        p.inc("so", 16)
    p0.close()
    RET = dict(dbg=dbg_t, din=din, out=out, nc=nc, p=p, ctx=ctx,
               scr=dict(q=scr_q, xmid=scr_xmid, core=scr_core, rows=scr_rows,
                        c10=scr_c10))
    cstage = None
    RET["sb"] = dict(ident=ident, identb=identb, permb=permb, masksb=masksb,
                     ropeC=ropeC, ropeS=ropeS, onesb=onesb, onesf=onesf,
                     cpool=cpool, modv=modv, bqkv_fm=bqkv_fm, bkfm=bkfm,
                     dbfm=dbfm, sb1fm=sb1fm, ln2ab=ln2ab, bvB=bvB, boutr=boutr,
                     ubcbr=ubcbr, bcastA=bcastA, rbadj=rbadj,
                     rwsb=rwsb, cstage=cstage,
                     gb1sb=gb1sb, b2s=b2s, sb2r=sb2r)
    RET["weights"] = dict(wqkv=wqkv, wout=wout_w, down=down_w, up=up_w,
                          core=core_w, sh1=shared_w1, sh2=shared_w2,
                          ew1=e_w1, eb1=e_b1, ew2=e_w2, eb2=e_b2,
                          x=x_in)
    RET["helpers"] = (W, WG, barrier, DBG, SB)
    RET["precast"] = (emit_pc, lambda: len(PRECAST), PC_TOTAL)
    RET["sb12"] = sb12
    RET["scr"]["e1b"], RET["scr"]["e2b"] = scr_e1b, scr_e2b
    RET["scr"]["s1b"], RET["scr"]["s2b"] = scr_s1b, scr_s2b
    RET["LATE"] = LATE
    if "p1" in PH:
        add_p12(p, SEM, RET, PH)
    if "p345" in PH:
        add_p345(p, SEM, RET)

    block = ctx.enter_context(nc.Block())
    block.sync(lambda e: [fn(e) for fn in p.q["sync"]])
    block.gpsimd(lambda e: [fn(e) for fn in p.q["gpsimd"]])
    block.vector(lambda e: [fn(e) for fn in p.q["vector"]])
    block.scalar(lambda e: [fn(e) for fn in p.q["scalar"]])
    def run_tensor(e):
        pe = TProxy(e)
        for fn in p.q["tensor"]:
            fn(pe)
    block.tensor(run_tensor)
    return nc, RET


from concourse.bass_utils import run_bass_kernel_spmd


def _core_inputs(inputs, core, con):
    m = {
        "x": np.ascontiguousarray(inputs["L0_attn_x"][core * 2:core * 2 + 2]),
        "c": np.ascontiguousarray(inputs["L0_c"][core * 2:core * 2 + 2]),
        "Wqkv": inputs["L0_attn_Wqkv"], "bqkv": inputs["L0_attn_bqkv"],
        "Wout": inputs["L0_attn_Wout"], "bout": inputs["L0_attn_bout"],
        "down_w": inputs["L0_ffn_down_w"], "down_b": inputs["L0_ffn_down_b"],
        "up_w": inputs["L0_ffn_up_w"], "up_b": inputs["L0_ffn_up_b"],
        "router_w": inputs["L0_ffn_router_w"], "router_b": inputs["L0_ffn_router_b"],
        "shared_w1": inputs["L0_ffn_shared_w1"], "shared_b1": inputs["L0_ffn_shared_b1"],
        "shared_w2": inputs["L0_ffn_shared_w2"], "shared_b2": inputs["L0_ffn_shared_b2"],
        "core_w": inputs["L0_ffn_core_w"], "core_b": inputs["L0_ffn_core_b"],
        "wmod": inputs["L0_wmod"],
    }
    for e in range(8):
        for part in ("w1", "b1", "w2", "b2"):
            m[f"e{e}_{part}"] = inputs[f"L0_ffn_e{e}_{part}"]
    m.update(con)
    return m


def kernel(**inputs):
    con = host_constants()
    nc, R = build()
    in_maps = [_core_inputs(inputs, c, con) for c in range(8)]
    res = run_bass_kernel_spmd(nc, in_maps, core_ids=list(range(8)))
    out = np.concatenate([res.results[c]["out"] for c in range(8)], axis=0)
    return out.reshape(16, 2048, 2048)


# version 13
# speedup vs baseline: 1.0154x; 1.0154x over previous
import math
import numpy as np
from contextlib import ExitStack

import concourse.bass as bass
import concourse.mybir as mybir

F32 = mybir.dt.float32
BF16 = mybir.dt.bfloat16
AF = mybir.ActivationFunctionType
ALU = mybir.AluOpType
AX = mybir.AxisListType

S, D, NH, NKV, HD = 2048, 2048, 32, 8, 64
QKV = (NH + 2 * NKV) * HD
L = 1024
EHID = [1024, 2048, 3072, 4096, 1024, 2048, 3072, 4096]
NSEQ = 2
T = NSEQ * S
DC, LCH = D // 128, L // 128
TWO_PI, PI = 2.0 * math.pi, math.pi
EPS = 1e-5
COST_LAMBDA = 1e-7
ROPE_BASE = 10000.0
NEG_LN_BASE_16 = -math.log(ROPE_BASE) / 16.0


def host_constants():
    con = {}
    con["ident_f32"] = np.eye(128, dtype=np.float32)
    perm = np.zeros((128, 128), dtype=np.float32)
    for m in range(128):
        lm = m % 64
        b = m - lm
        src = b + lm + 16 if lm < 16 else (b + lm - 16 if lm < 32 else m)
        perm[src, m] = 1.0
    con["rope_perm"] = perm
    ks = np.arange(128)[:, None]
    qs = np.arange(512)[None, :]
    con["causal_masks"] = np.concatenate(
        [(ks <= a + qs).astype(np.float32) for a in (0, -128, -256, -384)], axis=1)
    j = np.arange(16, dtype=np.float64)
    invfreq = 1.0 / ROPE_BASE ** (2 * j / 32)
    t = np.arange(S, dtype=np.float64)
    theta = t[None, :] * invfreq[:, None]
    con["n_sin"] = np.round(theta / TWO_PI).astype(np.float32)
    con["n_cos"] = np.round((theta + PI / 2) / TWO_PI).astype(np.float32)
    con["router_cost"] = np.array([[L * h + h * L for h in EHID]], dtype=np.float32)
    selc = np.zeros((16, 128), dtype=np.float32)
    sel1 = np.zeros((1, 128), dtype=np.float32)
    sels = np.zeros((16, 128), dtype=np.float32)
    for r in range(128):
        lm = r % 64
        if lm < 32:
            selc[lm % 16, r] = 1.0
            sels[lm % 16, r] = -1.0 if lm < 16 else 1.0
        else:
            sel1[0, r] = 1.0
    con["rope_selc"] = selc
    con["rope_sel1"] = sel1
    con["rope_sels"] = sels
    return con


class P:
    ENGS = ("sync", "gpsimd", "vector", "scalar", "tensor")

    def __init__(self, nc, ctx):
        self.nc, self.ctx = nc, ctx
        self.q = {e: [] for e in self.ENGS}
        self.sems, self.cnt = {}, {}
        self.freed = {}

    def sem(self, name):
        h = self.ctx.enter_context(self.nc.semaphore(name))
        self.sems[name], self.cnt[name] = h, 0
        return h

    def on(self, eng):
        def deco(fn):
            self.q[eng].append(fn)
        return deco

    def inc(self, name, by=1):
        self.cnt[name] += by
        return self.cnt[name]

    def free_at(self, buf, semname, value):
        self.freed[buf] = (semname, value)

    def guard(self, buf):
        return self.freed.get(buf, (None, 0))


class _TDef:

    def __init__(self, px):
        self.px = px

    def then_inc(self, sem, n):
        px = self.px
        if px._last is not None and not px._used:
            px._used = True
            return px._last.then_inc(sem, n)
        return px._e.drain().then_inc(sem, n)


class TProxy:
    def __init__(self, e):
        self._e, self._last, self._used = e, None, False

    def __getattr__(self, k):
        return getattr(self._e, k)

    def matmul(self, *a, **kw):
        r = self._e.matmul(*a, **kw)
        self._last, self._used = r, False
        return r

    def transpose(self, *a, **kw):
        r = self._e.transpose(*a, **kw)
        self._last, self._used = r, False
        return r

    def drain(self):
        return _TDef(self)


def add_p12(p, SEM, R, PH):
    nc = p.nc
    din, sb, scr = R["din"], R["sb"], R["scr"]
    W_, WG, barrier, DBG, SB = R["helpers"]
    emit_pc, pc_left, PC_TOTAL = R["precast"]
    x_in, wqkv, wout_w = din["x"], din["Wqkv"], din["Wout"]
    scr_q, scr_xmid, scr_rows = scr["q"], scr["xmid"], scr["rows"]
    ident, identb, permb = sb["ident"], sb["identb"], sb["permb"]
    masksb, ropeC, ropeS = sb["masksb"], sb["ropeC"], sb["ropeS"]
    onesb, onesf = sb["onesb"], sb["onesf"]
    modv, bqkv_fm, bkfm = sb["modv"], sb["bqkv_fm"], sb["bkfm"]
    ln2ab, bvB, boutr, bcastA = sb["ln2ab"], sb["bvB"], sb["boutr"], sb["bcastA"]
    cpool = sb["cpool"]

    def W(e, s, v):
        if v > 0:
            e.wait_ge(SEM[s], v)

    do_p2 = "p2" in PH

    stKV = ExitStack()
    Khat = stKV.enter_context(nc.sbuf_tensor("Khat", [65, NKV * S], BF16))
    Vhat = stKV.enter_context(nc.sbuf_tensor("Vhat", [128, 16 * 520], BF16))
    vst = stKV.enter_context(nc.sbuf_tensor("vst", [128, 1], F32))
    ainv = stKV.enter_context(nc.sbuf_tensor("ainv", [128, 1], F32))
    bneg = stKV.enter_context(nc.sbuf_tensor("bneg", [128, 1], F32))
    stt_ = stKV.enter_context(nc.sbuf_tensor("stt", [128, 24], F32))
    agg = stKV.enter_context(nc.sbuf_tensor("agg", [128, 2], F32))

    pp = ExitStack()
    pT2 = [pp.enter_context(nc.psum_tensor(f"pT2{i}", [128, 1024], F32)) for i in range(2)]
    pS = [pp.enter_context(nc.psum_tensor(f"pS{i}", [128, 512], F32)) for i in range(2)]
    pO2 = [pp.enter_context(nc.psum_tensor(f"pO2{i}", [128, 512], F32)) for i in range(2)]

    def emit_seq(s_):
        soff = s_ * 4 * DC

        st1 = ExitStack()
        T1 = lambda n, shp, dt: st1.enter_context(nc.sbuf_tensor(n + str(s_), list(shp), dt))
        tb = T1("tb", (128, D), F32)
        hfm = T1("hfm", (128, DC * S), BF16)
        wband = [T1(f"wband{i}", (128, DC * 128), BF16) for i in range(2)]
        wk32 = T1("wk32", (128, DC * 64), F32)
        wvb = T1("wvb", (128, DC * 512), BF16)
        xb = [T1("xb0", (128, D), F32), wvb[:, 0:2 * D].bitcast(F32)]
        qraw = T1("qraw", (128, 512), BF16)
        qt2 = T1("qt2", (128, 512), BF16)
        qfin = T1("qfin", (128, 512), BF16)

        ev_done = p.cnt["vv"]
        xg = {0: 0, 1: 0}
        for i in range(16):
            slot = i % 2
            ssl = "xs0" if slot == 0 else "xs1"

            @p.on("sync")
            def f(e, s_=s_, i=i, ssl=ssl, slot=slot, g=xg[slot]):
                if g:
                    W(e, "vv", g)
                e.dma_start(out=xb[slot][:], in_=x_in[s_, i * 128:(i + 1) * 128, :])\
                 .then_inc(SEM[ssl], 16)
            xtgt = p.inc(ssl, 16)

            @p.on("vector")
            def f(e, slot=slot, ssl=ssl, xtgt=xtgt, xb=xb):
                W(e, ssl, xtgt)
                for k4 in range(4):
                    e.bn_stats(stt_[:, k4 * 6:(k4 + 1) * 6],
                               xb[slot][:, k4 * 512:(k4 + 1) * 512])
                e.drain()
                e.bn_aggr(agg[:], stt_[:])
                e.drain()
                e.tensor_scalar(vst[:], agg[:, 1:2], EPS, None, op0=ALU.add)
                e.drain()
                e.drain().then_inc(SEM["vv"], 1)
            vv_st = p.inc("vv")

            @p.on("scalar")
            def f(e, vv_st=vv_st):
                W(e, "vv", vv_st)
                e.activation(vst[:], vst[:], AF.Sqrt).then_inc(SEM["sc"], 1)
            sc_sq = p.inc("sc")

            @p.on("vector")
            def f(e, slot=slot, sc_sq=sc_sq, xb=xb, tb=tb):
                W(e, "sc", sc_sq)
                e.reciprocal(ainv[:], vst[:])
                e.tensor_scalar(bneg[:], agg[:, 0:1], -1.0, None, op0=ALU.mult)
                e.drain()
                e.tensor_scalar(tb[:], xb[slot][:], bneg[:], ainv[:],
                                op0=ALU.add, op1=ALU.mult)
                e.drain()
                e.drain().then_inc(SEM["vv"], 1)
            vv_nm = p.inc("vv")
            xg[slot] = vv_nm

            @p.on("tensor")
            def f(e, vv_nm=vv_nm, evg=ev_done, tb=tb):
                W(e, "vv", max(vv_nm, evg))
                for kc in range(DC):
                    e.transpose(pT2[kc // 8][0:128, (kc % 8) * 128:(kc % 8) * 128 + 128],
                                tb[:, kc * 128:(kc + 1) * 128], ident[:])
                e.drain().then_inc(SEM["tv"], 1)
            tv_tr = p.inc("tv")

            @p.on("vector")
            def f(e, i=i, soff=soff, tv_tr=tv_tr, hfm=hfm):
                W(e, "tv", tv_tr)
                g512, tin = divmod(i, 4)
                for kc in range(DC):
                    base = kc * S + g512 * 512 + tin * 128
                    e.tensor_scalar(hfm[:, base:base + 128],
                                    pT2[kc // 8][0:128, (kc % 8) * 128:(kc % 8) * 128 + 128],
                                    modv[:, soff + DC + kc:soff + DC + kc + 1],
                                    modv[:, soff + kc:soff + kc + 1],
                                    op0=ALU.mult, op1=ALU.add)
                e.drain()
                e.drain().then_inc(SEM["vv"], 1)
            ev_done = p.inc("vv")

        barrier()
        d = DBG(f"hfm{s_}", (128, DC * S), BF16)
        if d is not None:
            @p.on("sync")
            def f(e, d=d, hfm=hfm):
                e.dma_start(out=d, in_=hfm[:]).then_inc(SEM["so"], 16)
            p.inc("so", 16)

        @p.on("vector")
        def f(e):
            e.memset(Khat[64:65, :], -1.0)
            e.memset(Vhat[:].rearrange("p (t h j) -> p t h j", h=NKV, j=65)
                     [:, :, :, 64:65], 1.0)
            e.drain().then_inc(SEM["vv"], 1)
        p.inc("vv")

        psfree = [p.cnt["vv"], p.cnt["vv"]]
        ps2free = p.cnt["vv"]
        wslot_guard = [0, 0]
        wk32_guard = 0
        q_store_guard = 0
        for ci in range(24):
            isq = ci < 16
            slot = ci % 2
            band, ssem = wband[slot], ("ws0" if slot == 0 else "ws1")
            if isq:
                colbeg, colw = ci * 128, 128
            else:
                kvh = ci - 16
                colbeg, colw = 2048 + kvh * 64, 64

            @p.on("gpsimd")
            def f(e, colbeg=colbeg, colw=colw, band=band, ssem=ssem, isq=isq,
                  g=wslot_guard[slot], g2=wk32_guard, wk32=wk32):
                W(e, "tv", g)
                if not isq:
                    W(e, "vv", g2)
                dst = (band[:, 0:DC * colw] if isq else wk32[:, 0:DC * colw])\
                    .rearrange("p (c f) -> p c f", c=DC)
                e.dma_start(out=dst,
                            in_=wqkv[:, colbeg:colbeg + colw]
                            .rearrange("(c p) f -> p c f", p=128)).then_inc(SEM[ssem], 16)
            wtgt = p.inc(ssem, 16)
            emit_pc(1)
            kv = None
            if not isq:
                @p.on("vector")
                def f(e, band=band, ssem=ssem, wtgt=wtgt, colw=colw, wk32=wk32):
                    W(e, ssem, wtgt)
                    e.tensor_scalar(band[:, 0:DC * colw], wk32[:, 0:DC * colw],
                                    0.125, None, op0=ALU.mult)
                    e.drain()
                    e.drain().then_inc(SEM["vv"], 1)
                kv = p.inc("vv")
                wk32_guard = kv

            for grp in range(4):
                g = grp % 2
                @p.on("tensor")
                def f(e, grp=grp, g=g, band=band, ssem=ssem, wtgt=wtgt, kv=kv,
                      colw=colw, fr=psfree[g], hfm=hfm):
                    if grp == 0:
                        W(e, ssem, wtgt)
                        if kv is not None:
                            W(e, "vv", kv)
                    W(e, "vv", fr)
                    for kc in range(DC):
                        e.matmul(pS[g][0:colw, 0:512],
                                 band[:, kc * colw:(kc + 1) * colw],
                                 hfm[:, kc * S + grp * 512:kc * S + grp * 512 + 512],
                                 start=(kc == 0), stop=(kc == DC - 1))
                    e.drain().then_inc(SEM["tv"], 1)
                tv_mm = p.inc("tv")
                if grp == 3:
                    wslot_guard[slot] = tv_mm

                @p.on("vector")
                def f(e, ci=ci, g=g, tv_mm=tv_mm, colw=colw, isq=isq, qraw=qraw):
                    W(e, "tv", tv_mm)
                    bias = (bqkv_fm[:, ci:ci + 1] if isq
                            else bkfm[:, ci - 16:ci - 16 + 1])
                    e.tensor_scalar(qraw[0:colw, :], pS[g][0:colw, 0:512], bias,
                                    None, op0=ALU.add)
                    e.drain()
                    e.drain().then_inc(SEM["vv"], 1)
                vv_ev = p.inc("vv")
                psfree[g] = vv_ev

                @p.on("tensor")
                def f(e, vv_ev=vv_ev, colw=colw, fr=ps2free, qraw=qraw):
                    W(e, "vv", max(vv_ev, fr))
                    e.matmul(pO2[0][0:colw, 0:512], permb[0:colw, 0:colw],
                             qraw[0:colw, 0:512], start=True, stop=True)
                    e.drain().then_inc(SEM["tv"], 1)
                tv_pm = p.inc("tv")

                @p.on("vector")
                def f(e, ci=ci, grp=grp, tv_pm=tv_pm, colw=colw, isq=isq,
                      qraw=qraw, qt2=qt2, qfin=qfin, qsg=q_store_guard):
                    W(e, "tv", tv_pm)
                    if qsg:
                        W(e, "qs0", qsg)
                    e.tensor_copy(qt2[0:colw, :], pO2[0][0:colw, 0:512])
                    e.drain()
                    e.tensor_tensor(qt2[0:colw, :], qt2[0:colw, :],
                                    ropeS[0:colw, grp * 512:(grp + 1) * 512], op=ALU.mult)
                    e.drain()
                    e.tensor_tensor(qfin[0:colw, :], qraw[0:colw, :],
                                    ropeC[0:colw, grp * 512:(grp + 1) * 512], op=ALU.mult)
                    e.drain()
                    if isq:
                        e.tensor_tensor(qfin[0:colw, :], qfin[0:colw, :],
                                        qt2[0:colw, :], op=ALU.add)
                    else:
                        kvh = ci - 16
                        base = kvh * S + grp * 512
                        e.tensor_tensor(Khat[0:64, base:base + 512],
                                        qfin[0:64, :], qt2[0:64, :], op=ALU.add)
                    e.drain()
                    e.drain().then_inc(SEM["vv"], 1)
                vv_rp = p.inc("vv")
                ps2free = vv_rp

                if isq:
                    @p.on("sync")
                    def f(e, ci=ci, grp=grp, vv_rp=vv_rp, qfin=qfin):
                        W(e, "vv", vv_rp)
                        for hh in range(2):
                            h = ci * 2 + hh
                            e.dma_start(
                                out=scr_q[0:64, h * S + grp * 512:h * S + grp * 512 + 512],
                                in_=qfin[hh * 64:(hh + 1) * 64, :]).then_inc(SEM["qs0"], 16)
                    q_store_guard = p.inc("qs0", 32)

        @p.on("gpsimd")
        def f(e, g=max(wslot_guard), wvb=wvb):
            W(e, "tv", g)
            e.dma_start(out=wvb[:].rearrange("p (c f) -> p c f", c=DC),
                        in_=wqkv[:, 2560:3072].rearrange("(c p) f -> p c f", p=128))\
             .then_inc(SEM["w20"], 16)
        w2tgt = p.inc("w20", 16)

        for tc in range(16):
            g = tc % 2
            @p.on("tensor")
            def f(e, tc=tc, g=g, w2tgt=w2tgt, fr=psfree[g], hfm=hfm, wvb=wvb):
                if tc == 0:
                    W(e, "w20", w2tgt)
                W(e, "vv", fr)
                for kc in range(DC):
                    e.matmul(pS[g][0:128, 0:512],
                             hfm[:, kc * S + tc * 128:kc * S + tc * 128 + 128],
                             wvb[:, kc * 512:(kc + 1) * 512],
                             start=(kc == 0), stop=(kc == DC - 1))
                e.drain().then_inc(SEM["tv"], 1)
            tv_v = p.inc("tv")

            @p.on("vector")
            def f(e, tc=tc, g=g, tv_v=tv_v):
                W(e, "tv", tv_v)
                dst = Vhat[:, tc * 520:(tc + 1) * 520]\
                    .rearrange("p (h j) -> p h j", j=65)[:, :, 0:64]
                src = pS[g][0:128, 0:512].rearrange("p (h j) -> p h j", j=64)
                bb = bvB[:].rearrange("p (h j) -> p h j", j=64)
                e.tensor_tensor(dst, src, bb, op=ALU.add)
                e.drain()
                e.drain().then_inc(SEM["vv"], 1)
            psfree[g] = p.inc("vv")

        barrier()
        st1.close()
        d = DBG(f"khat{s_}", (65, NKV * S), BF16)
        if d is not None:
            @p.on("sync")
            def f(e, d=d):
                e.dma_start(out=d, in_=Khat[:]).then_inc(SEM["so"], 16)
            p.inc("so", 16)
        d = DBG(f"vhat{s_}", (128, 16 * 520), BF16)
        if d is not None:
            @p.on("sync")
            def f(e, d=d):
                e.dma_start(out=d, in_=Vhat[:]).then_inc(SEM["so"], 16)
            p.inc("so", 16)

        if not do_p2:
            return

        st2 = ExitStack()
        T2 = lambda n, shp, dt: st2.enter_context(nc.sbuf_tensor(n + str(s_), list(shp), dt))
        Qq = T2("Qq", (65, NH * 512), BF16)
        Ofm = T2("Ofm", (128, DC * 512), BF16)
        u16 = T2("u16", (128, DC * 512), BF16)
        u16f = u16[:].bitcast(F32)
        PT = [u16[:, 0:1024], u16[:, 1024:2048]]
        PTm = u16[:, 2048:3072]
        otmp = [u16[0:64, 3072:3584], u16[0:64, 3584:4096]]
        absq = [u16[0:64, 7168:7680], u16[0:64, 7680:8192]]
        rtmp = u16f[:, 2048:2560]
        rzB = [u16f[0:64, 2560:3072], u16f[0:64, 3072:3584]]
        rzs = [T2(f"rzs{i}", (1, 512), F32) for i in range(2)]
        cmax = T2("cmax", (64, 8), BF16)
        wob = T2("wob", (128, DC * 512), BF16)
        xmb = T2("xmb", (128, 4 * D), F32)

        @p.on("sync")
        def f(e, s_=s_):
            e.dma_start(out=bcastA[:],
                        in_=scr_rows[s_ * 2, 0:2048].unsqueeze(0)
                        .partition_broadcast(128)).then_inc(SEM["sx"], 16)
        sx_ga = p.inc("sx", 16)

        @p.on("vector")
        def f(e):
            for kvh in range(NKV):
                e.reduce_max(cmax[:, kvh:kvh + 1], Khat[0:64, kvh * S:(kvh + 1) * S],
                             axis=AX.X, apply_absolute_value=True)
            e.drain()
            e.drain().then_inc(SEM["vv"], 1)
        vv_cm = p.inc("vv")

        pofree = {0: p.cnt["vv"], 1: p.cnt["vv"]}
        otg = {0: 0, 1: 0}
        qq_g = [p.cnt["tv"]]
        WOL = [0]
        mh_mm = {0: 0, 1: 0}
        mh_ev = {0: 0, 1: 0}
        for qg in range(4):
            @p.on("sync")
            def f(e, s_=s_, qg=qg, g=qq_g[0]):
                W(e, "tv", g)
                e.dma_start(out=Qq[0:64, :].rearrange("k (h s) -> k h s", s=512),
                            in_=scr_q.rearrange("k (h s) -> k h s", h=NH)
                            [:, :, qg * 512:(qg + 1) * 512]).then_inc(SEM["sx"], 16)
            sx_q = p.inc("sx", 16)

            for h in range(NH):
                kvh = h // 4
                sl = h % 2
                @p.on("scalar")
                def f(e, h=h, sl=sl, sx_q=sx_q, g=mh_mm[sl]):
                    if h == 0:
                        W(e, "sx", sx_q)
                    if g:
                        W(e, "tv", g)
                    e.activation(absq[sl][:], Qq[0:64, h * 512:(h + 1) * 512],
                                 AF.Abs).then_inc(SEM["sc"], 1)
                sc_a = p.inc("sc")

                @p.on("tensor")
                def f(e, kvh=kvh, sl=sl, sc_a=sc_a, vv_cm=vv_cm,
                      fr=(ps2free if h < 2 else mh_ev[sl])):
                    W(e, "sc", sc_a)
                    W(e, "vv", max(vv_cm, fr))
                    e.matmul(pT2[0][0:1, sl * 512:(sl + 1) * 512],
                             cmax[:, kvh:kvh + 1], absq[sl][:],
                             start=True, stop=True)
                    e.drain().then_inc(SEM["tv"], 1)
                mh_mm[sl] = p.inc("tv")

                @p.on("vector")
                def f(e, h=h, sl=sl, tv_mh=mh_mm[sl]):
                    W(e, "tv", tv_mh)
                    e.tensor_copy(Qq[64:65, h * 512:(h + 1) * 512],
                                  pT2[0][0:1, sl * 512:(sl + 1) * 512])
                    e.drain().then_inc(SEM["vv"], 1)
                mh_ev[sl] = p.inc("vv")

            mh_done = p.cnt["vv"]

            npair = 2 * (qg + 1)
            exp_done = {0: 0, 1: 0}
            pend_tail = [None]

            def emit_scores(h, cp, mh_done=mh_done):
                par = cp % 2
                @p.on("tensor")
                def f(e, h=h, cp=cp, kvh=h // 4, par=par, g1=exp_done[par],
                      vvq=mh_done):
                    W(e, "sc", g1)
                    W(e, "vv", vvq)
                    for j in range(2):
                        kb = cp * 2 + j
                        e.matmul(pT2[par][0:128, j * 512:(j + 1) * 512],
                                 Khat[0:65, kvh * S + kb * 128:kvh * S + kb * 128 + 128],
                                 Qq[0:65, h * 512:(h + 1) * 512],
                                 start=True, stop=True)
                    e.drain().then_inc(SEM["tv"], 1)
                tv1 = p.inc("tv")
                return (tv1, tv1)

            for h in range(NH):
                kvh = h // 4
                po = pO2[h % 2]
                av_done = {0: 0, 1: 0}
                tv_s = {0: emit_scores(h, 0)}
                prev_vv_m = 0
                for cp in range(npair):
                    par = cp % 2
                    diag = (cp >= npair - 2)
                    sc_es = []
                    for j in range(2):
                        @p.on("scalar")
                        def f(e, j=j, tv_sc=tv_s[cp][j], par=par, diag=diag,
                              g=av_done[par], PT=PT, PTm=PTm,
                              vg=prev_vv_m if cp == npair - 1 and j == 0 else 0):
                            W(e, "tv", max(tv_sc, g))
                            if vg:
                                W(e, "vv", vg)
                            dst = PTm if diag else PT[par]
                            e.activation(dst[:, j * 512:(j + 1) * 512],
                                         pT2[par][0:128, j * 512:(j + 1) * 512],
                                         AF.Exp).then_inc(SEM["sc"], 1)
                        sc_es.append(p.inc("sc"))
                    sc_e = sc_es[1]
                    exp_done[par] = sc_e

                    vv_m = None
                    if diag:
                        jo = 0 if cp == npair - 2 else 2
                        @p.on("vector")
                        def f(e, par=par, jo=jo, sc_e=sc_e, g=av_done[par],
                              PT=PT, PTm=PTm):
                            W(e, "sc", sc_e)
                            W(e, "tv", g)
                            for j in range(2):
                                e.tensor_tensor(PT[par][:, j * 512:(j + 1) * 512],
                                                PTm[:, j * 512:(j + 1) * 512],
                                                masksb[:, (jo + j) * 512:(jo + j + 1) * 512],
                                                op=ALU.mult)
                            e.drain().then_inc(SEM["vv"], 1)
                        vv_m = p.inc("vv")
                        prev_vv_m = vv_m

                    if cp + 1 < npair:
                        tv_s[cp + 1] = emit_scores(h, cp + 1)

                    for j in range(2):
                        @p.on("tensor")
                        def f(e, cp=cp, npair=npair, kvh=kvh, par=par, po=po,
                              j=j, scw=sc_es[j], vv_m=vv_m, frO=pofree[h % 2],
                              h=h, PT=PT):
                            if vv_m is not None:
                                if j == 0:
                                    W(e, "vv", vv_m)
                            else:
                                W(e, "sc", scw)
                            if cp == 0 and j == 0:
                                W(e, "vv", frO)
                            kb = cp * 2 + j
                            e.matmul(po[0:65, 0:512],
                                     Vhat[:, kb * 520 + kvh * 65:kb * 520 + kvh * 65 + 65],
                                     PT[par][:, j * 512:(j + 1) * 512],
                                     start=(cp == 0 and j == 0),
                                     stop=(cp == npair - 1 and j == 1))
                            e.drain().then_inc(SEM["tv"], 1)
                        p.inc("tv")
                    av_done[par] = p.cnt["tv"]

                    if cp == 1 and pend_tail[0] is not None:
                        pend_tail[0]()
                        pend_tail[0] = None

                hs2 = h % 2
                @p.on("vector")
                def f(e, po=po, hs2=hs2, g=max(av_done.values())):
                    W(e, "tv", g)
                    e.reciprocal(rzs[hs2][:], po[64:65, 0:512])
                    e.drain().then_inc(SEM["vv"], 1)
                vv_rz = p.inc("vv")

                def tail(h=h, hs2=hs2, po=po, vv_rz=vv_rz):
                    nonlocal ps2free
                    @p.on("tensor")
                    def f(e, hs2=hs2, vv_rz=vv_rz, fr=psfree[hs2]):
                        W(e, "vv", max(vv_rz, fr))
                        e.matmul(pS[hs2][0:64, 0:512], onesf[0:1, 0:64], rzs[hs2][:],
                                 start=True, stop=True)
                        e.drain().then_inc(SEM["tv"], 1)
                    tv_rb = p.inc("tv")

                    @p.on("vector")
                    def f(e, h=h, hs2=hs2, po=po, tv_rb=tv_rb, Ofm=Ofm,
                          go=otg[hs2], wog=WOL[0]):
                        W(e, "tv", max(tv_rb, wog))
                        W(e, "sx", go)
                        e.tensor_copy(rzB[hs2][:], pS[hs2][0:64, 0:512])
                        e.drain()
                        if h % 2 == 0:
                            e.tensor_tensor(Ofm[0:64, (h // 2) * 512:(h // 2) * 512 + 512],
                                            po[0:64, 0:512], rzB[hs2][:], op=ALU.mult)
                        else:
                            e.tensor_tensor(otmp[hs2][:], po[0:64, 0:512], rzB[hs2][:],
                                            op=ALU.mult)
                        e.drain().then_inc(SEM["vv"], 1)
                    vv_oe = p.inc("vv")
                    ps2free = vv_oe
                    pofree[h % 2] = vv_oe
                    psfree[hs2] = vv_oe

                    if h % 2 == 1:
                        @p.on("sync")
                        def f(e, h=h, hs2=hs2, vv_oe=vv_oe, Ofm=Ofm):
                            W(e, "vv", vv_oe)
                            e.dma_start(out=Ofm[64:128, (h // 2) * 512:(h // 2) * 512 + 512],
                                        in_=otmp[hs2][:]).then_inc(SEM["sx"], 16)
                        sx_ot = p.inc("sx", 16)
                        otg[hs2] = sx_ot

                pend_tail[0] = tail

            pend_tail[0]()
            pend_tail[0] = None
            qq_g[0] = p.cnt["tv"]

            @p.on("vector")
            def f(e, g=max(otg.values()) if otg else 0):
                W(e, "sx", g)
                e.drain().then_inc(SEM["vv"], 1)
            p.inc("vv")

            ofm_ready = p.cnt["vv"]

            @p.on("sync")
            def f(e, s_=s_, qg=qg, g=p.cnt["vv"], gq=p.cnt["qs0"], xmb=xmb):
                W(e, "vv", g)
                W(e, "qs0", gq)
                e.dma_start(out=xmb[:].rearrange("p (t d) -> p t d", t=4),
                            in_=x_in[s_, qg * 512:(qg + 1) * 512, :]
                            .rearrange("(t p) d -> p t d", p=128)).then_inc(SEM["sx"], 16)
            sx_x = p.inc("sx", 16)

            woguard = WOL[0]
            for dg in range(4):
                @p.on("gpsimd")
                def f(e, dg=dg, g=woguard, wob=wob):
                    W(e, "tv", g)
                    e.dma_start(out=wob[:, :].rearrange("p (c f) -> p c f", c=DC),
                                in_=wout_w[:, dg * 512:(dg + 1) * 512]
                                .rearrange("(c p) f -> p c f", p=128))\
                     .then_inc(SEM["w21"], 16)
                wotgt = p.inc("w21", 16)

                for tcl in range(4):
                    g = tcl % 2
                    @p.on("tensor")
                    def f(e, dg=dg, tcl=tcl, g=g, wotgt=wotgt, fr=psfree[g],
                          ofr=ofm_ready, Ofm=Ofm, wob=wob):
                        if tcl == 0:
                            W(e, "w21", wotgt)
                            W(e, "vv", ofr)
                        W(e, "vv", fr)
                        for fc in range(DC):
                            e.matmul(pS[g][0:128, 0:512],
                                     Ofm[:, fc * 512 + tcl * 128:fc * 512 + tcl * 128 + 128],
                                     wob[:, fc * 512:(fc + 1) * 512],
                                     start=(fc == 0), stop=False)
                        e.matmul(pS[g][0:128, 0:512], onesb[0:1, 0:128],
                                 boutr[0:1, dg * 512:(dg + 1) * 512],
                                 start=False, stop=True)
                        e.drain().then_inc(SEM["tv"], 1)
                    tv_w = p.inc("tv")
                    woguard = tv_w
                    WOL[0] = tv_w

                    @p.on("vector")
                    def f(e, dg=dg, tcl=tcl, g=g, tv_w=tv_w, sx_ga=sx_ga,
                          sx_x=sx_x, xmb=xmb, rtmp=rtmp):
                        W(e, "tv", tv_w)
                        W(e, "sx", max(sx_ga, sx_x))
                        e.tensor_tensor(rtmp[:], pS[g][0:128, 0:512],
                                        bcastA[:, dg * 512:(dg + 1) * 512], op=ALU.mult)
                        e.drain()
                        base = tcl * D + dg * 512
                        e.tensor_tensor(xmb[:, base:base + 512],
                                        xmb[:, base:base + 512], rtmp[:], op=ALU.add)
                        e.drain()
                        e.drain().then_inc(SEM["vv"], 1)
                    psfree[g] = p.inc("vv")

            for tcl in range(4):
                tile = s_ * 16 + qg * 4 + tcl

                @p.on("vector")
                def f(e, tcl=tcl, g=p.cnt["vv"], xmb=xmb):
                    W(e, "vv", g)
                    for k4 in range(4):
                        e.bn_stats(stt_[:, k4 * 6:(k4 + 1) * 6],
                                   xmb[:, tcl * D + k4 * 512:tcl * D + (k4 + 1) * 512])
                    e.drain()
                    e.bn_aggr(agg[:], stt_[:])
                    e.drain()
                    e.tensor_scalar(vst[:], agg[:, 1:2], EPS, None, op0=ALU.add)
                    e.drain()
                    e.drain().then_inc(SEM["vv"], 1)
                vv_st = p.inc("vv")

                @p.on("scalar")
                def f(e, vv_st=vv_st):
                    W(e, "vv", vv_st)
                    e.activation(vst[:], vst[:], AF.Sqrt).then_inc(SEM["sc"], 1)
                sc_sq = p.inc("sc")

                @p.on("vector")
                def f(e, tile=tile, sc_sq=sc_sq):
                    W(e, "sc", sc_sq)
                    e.reciprocal(ln2ab[:, 2 * tile + 1:2 * tile + 2], vst[:])
                    e.tensor_scalar(ln2ab[:, 2 * tile:2 * tile + 1], agg[:, 0:1],
                                    -1.0, None, op0=ALU.mult)
                    e.drain()
                    e.drain().then_inc(SEM["vv"], 1)
                vv_ab = p.inc("vv")

                @p.on("sync")
                def f(e, s_=s_, qg=qg, tcl=tcl, vv_ab=vv_ab, xmb=xmb):
                    W(e, "vv", vv_ab)
                    tok = s_ * S + qg * 512 + tcl * 128
                    e.dma_start(out=scr_xmid[tok:tok + 128, :],
                                in_=xmb[:, tcl * D:(tcl + 1) * D]).then_inc(SEM["qs0"], 16)
            p.inc("qs0", 16 * 4)

        barrier()
        st2.close()

    for _s in range(NSEQ):
        emit_seq(_s)
    emit_pc(10 ** 9)

    d = DBG("xmid", (T, D))
    if d is not None:
        @p.on("sync")
        def f(e, d=d):
            e.dma_start(out=d, in_=scr_xmid).then_inc(SEM["so"], 16)
        p.inc("so", 16)

    stKV.close()
    pp.close()
    R["sb12"].close()
    R["p2state"] = {}


def add_p345(p, SEM, R):
    nc = p.nc
    din, sb, scr = R["din"], R["sb"], R["scr"]
    W_, WG, barrier, DBG, SB = R["helpers"]
    out = R["out"]
    scr_xmid, scr_core, scr_rows, scr_c10 = scr["xmid"], scr["core"], scr["rows"], scr["c10"]
    down_w, up_w, core_w = din["down_w"], din["up_w"], din["core_w"]
    shared_w1, shared_w2 = scr["s1b"], scr["s2b"]
    emit_pc, pc_left, PC_TOTAL = R["precast"]
    ew1 = scr["e1b"]
    eb1 = [din[f"e{e}_b1"] for e in range(8)]
    ew2 = scr["e2b"]
    ident = sb["ident"]
    onesb, onesf = sb["onesb"], sb["onesf"]
    modv, dbfm, sb1fm = sb["modv"], sb["dbfm"], sb["sb1fm"]
    ln2ab, bcastA, ubcbr = sb["ln2ab"], sb["bcastA"], sb["ubcbr"]
    rwsb, gb1sb = sb["rwsb"], sb["gb1sb"]
    b2s, sb2r = sb["b2s"], sb["sb2r"]

    def W(e, s, v):
        if v > 0:
            e.wait_ge(SEM[s], v)

    def WT(e, *guards):
        for g in guards:
            if g is not None and g[1] > 0:
                e.wait_ge(SEM[g[0]], g[1])

    st = ExitStack()
    T_ = lambda n, shp, dt: st.enter_context(nc.sbuf_tensor(n, list(shp), dt))
    U = T_("U", (128, 32768), BF16)
    Uf = U[:].bitcast(F32)
    xmb = [Uf[:, 0:2048], Uf[:, 2048:4096]]
    h2 = U[:, 8192:16384]
    g2 = U[:, 16384:24576]
    cwb = U[:, 24576:32768]
    hid = U[:, 0:32768]
    gbufs = [T_("gbuf", (128, LCH * 1024), BF16),
             T_("gbuf2", (128, LCH * 1024), BF16)]
    moe = T_("moe", (128, LCH * 1024), F32)
    wA = T_("wA", (128, 4096), BF16)
    wB = T_("wB", (128, 4096), BF16)
    wC = T_("wC", (128, 4096), BF16)
    wD = T_("wD", (128, 4096), BF16)
    w1s = [wA[:, 0:4096], wB[:, 0:4096]]
    w2s = [wC[:, 0:4096], wD[:, 0:4096]]
    dwn = wB[:, 0:2048]
    ups2 = [wC[:, 0:4096], wD[:, 0:4096]]
    ctmp = T_("ctmp", (128, 512), F32)
    xms = [T_(f"xms{i}", (128, 512), F32)[:] for i in range(2)]
    cbs = [T_(f"cbs{i}", (128, 512), F32)[:] for i in range(2)]
    fins = [T_(f"fins{i}", (128, 512), F32)[:] for i in range(2)]
    webb = T_("webb", (128, 1024), BF16)
    b1f = T_("b1f", (128, 32), F32)
    ltmp = T_("ltmp", (8, 512), F32)
    rbadjB = T_("rbadjB", (128, 8), F32)
    rm = T_("rm", (128, 1), F32)
    rz_ = T_("rz_", (128, 1), F32)
    rex = T_("rex", (128, 8), F32)
    srt = T_("srt", (128, 8), F32)
    rg0 = T_("rg0", (128, 1), F32)
    rg1 = T_("rg1", (128, 1), F32)
    rt0 = T_("rt0", (128, 8), F32)
    w10s = T_("w10s", (128, 10), F32)
    c10sb = T_("c10sb", (10, L), BF16)
    w10fm = T_("w10fm", (10, 1024), BF16)
    logit_s = T_("logit_s", (128, 8 * 8), F32)
    ct8 = Uf[0:8, 6144:7168]
    ctrow = Uf[0:1, 7168:8192]
    one8 = T_("one8", (8, 1), F32)

    pp = ExitStack()
    pZ = [pp.enter_context(nc.psum_tensor(f"pZ{i}", [128, 1024], F32)) for i in range(2)]
    pW = [pp.enter_context(nc.psum_tensor(f"pW{i}", [128, 512], F32)) for i in range(4)]

    @p.on("sync")
    def f(e, w=p.cnt["sx"] + 16):
        e.dma_start(out=scr_rows[15:16, 0:8], in_=sb["rbadj"][:]).then_inc(SEM["sx"], 16)
        e.wait_ge(SEM["sx"], w)
        e.dma_start(out=rbadjB[:], in_=scr_rows[15, 0:8].unsqueeze(0)
                    .partition_broadcast(128)).then_inc(SEM["sx"], 16)
    p.inc("sx", 16)
    sx_rbB = p.inc("sx", 16)

    @p.on("vector")
    def f(e):
        e.memset(one8[:], 1.0).then_inc(SEM["vv"], 1)
    p.inc("vv")

    p5xf = {0: 0, 1: 0}
    p5fso = {0: 0, 1: 0}
    p5upt = {}
    PG = {i: None for i in range(4)}
    GBF = {0: 0, 1: 0}
    SXGB = {}

    def emit_blk(blk, c10_ready, p5q):
        s_ = blk // 2
        soff = s_ * 4 * DC
        gbuf = gbufs[blk % 2]
        moebf = gbuf
        pg = PG
        zg = {i: ("sc", p.cnt["sc"]) for i in range(2)}
        w1g = {0: None, 1: None}
        w2g = {0: None, 1: None}

        def take(n=1):
            for _ in range(n):
                if not p5q:
                    break
                p5q.pop(0)()

        g2_done = None
        for grp in range(2):
            for ti in range(4):
                gt = blk * 8 + grp * 4 + ti
                slot = ti % 2
                ssl = "xs0" if slot == 0 else "xs1"

                @p.on("sync")
                def f(e, gt=gt, slot=slot, ssl=ssl, g=p.cnt["tv"]):
                    W(e, "tv", g)
                    e.dma_start(out=xmb[slot],
                                in_=scr_xmid[gt * 128:(gt + 1) * 128, :])\
                     .then_inc(SEM[ssl], 16)
                xtgt = p.inc(ssl, 16)

                @p.on("vector")
                def f(e, gt=gt, slot=slot, ssl=ssl, xtgt=xtgt):
                    W(e, ssl, xtgt)
                    e.tensor_scalar(xmb[slot], xmb[slot],
                                    ln2ab[:, 2 * gt:2 * gt + 1],
                                    ln2ab[:, 2 * gt + 1:2 * gt + 2],
                                    op0=ALU.add, op1=ALU.mult)
                    e.drain().then_inc(SEM["vv"], 1)
                vv_nm = p.inc("vv")

                @p.on("tensor")
                def f(e, slot=slot, vv_nm=vv_nm, g0=zg[0], g1=zg[1]):
                    W(e, "vv", vv_nm)
                    WT(e, g0, g1)
                    for kc in range(DC):
                        e.transpose(pZ[kc // 8][0:128, (kc % 8) * 128:(kc % 8) * 128 + 128],
                                    xmb[slot][:, kc * 128:(kc + 1) * 128], ident[:])
                    e.drain().then_inc(SEM["tv"], 1)
                tv_tr = p.inc("tv")

                @p.on("vector")
                def f(e, ti=ti, soff=soff, tv_tr=tv_tr):
                    W(e, "tv", tv_tr)
                    for kc in range(DC):
                        base = kc * 512 + ti * 128
                        e.tensor_scalar(h2[:, base:base + 128],
                                        pZ[kc // 8][0:128, (kc % 8) * 128:(kc % 8) * 128 + 128],
                                        modv[:, soff + 3 * DC + kc:soff + 3 * DC + kc + 1],
                                        modv[:, soff + 2 * DC + kc:soff + 2 * DC + kc + 1],
                                        op0=ALU.mult, op1=ALU.add)
                    e.drain().then_inc(SEM["vv"], 1)
                zg[0] = zg[1] = ("vv", p.inc("vv"))
                take(1)
            h2_done = p.cnt["vv"]

            @p.on("scalar")
            def f(e, h2_done=h2_done):
                W(e, "vv", h2_done)
                e.activation(g2, h2, AF.Gelu).then_inc(SEM["sc"], 1)
            g2_done = p.inc("sc")

            for lc in range(LCH):
                pw = lc % 2
                @p.on("gpsimd")
                def f(e, lc=lc, g=("tv", p.cnt["tv"])):
                    WT(e, g)
                    e.dma_start(out=dwn.rearrange("p (c f) -> p c f", f=128),
                                in_=down_w[:, lc * 128:(lc + 1) * 128]
                                .rearrange("(c p) f -> p c f", p=128))\
                     .then_inc(SEM["ws0"], 16)
                dw_t = p.inc("ws0", 16)

                @p.on("tensor")
                def f(e, lc=lc, pw=pw, dw_t=dw_t, h2_done=h2_done, fr=pg[pw]):
                    W(e, "ws0", dw_t)
                    if lc == 0:
                        W(e, "vv", h2_done)
                    WT(e, fr)
                    for kc in range(DC):
                        e.matmul(pW[pw][0:128, 0:512],
                                 dwn[:, kc * 128:(kc + 1) * 128],
                                 h2[:, kc * 512:(kc + 1) * 512],
                                 start=(kc == 0), stop=(kc == DC - 1))
                    e.drain().then_inc(SEM["tv"], 1)
                tv_d = p.inc("tv")

                @p.on("scalar")
                def f(e, lc=lc, grp=grp, pw=pw, tv_d=tv_d, gbf=GBF[blk % 2],
                      gbuf=gbuf):
                    W(e, "tv", max(tv_d, gbf))
                    e.activation(gbuf[:, lc * 1024 + grp * 512:lc * 1024 + grp * 512 + 512],
                                 pW[pw][0:128, 0:512], AF.Gelu,
                                 bias=dbfm[:, lc:lc + 1]).then_inc(SEM["sc"], 1)
                pg[pw] = ("sc", p.inc("sc"))
                take(1)

            @p.on("tensor")
            def f(e, h2_done=h2_done, fr=pg[2]):
                W(e, "vv", h2_done)
                WT(e, fr)
                for kc in range(DC):
                    e.matmul(pW[2][0:8, 0:512], rwsb[:, kc * 8:(kc + 1) * 8],
                             h2[:, kc * 512:(kc + 1) * 512],
                             start=(kc == 0), stop=(kc == DC - 1))
                e.drain().then_inc(SEM["tv"], 1)
            tv_r = p.inc("tv")

            @p.on("vector")
            def f(e, tv_r=tv_r):
                W(e, "tv", tv_r)
                e.tensor_copy(ltmp[:], pW[2][0:8, 0:512])
                e.drain().then_inc(SEM["vv"], 1)
            vv_lt = p.inc("vv")
            pg[2] = ("vv", vv_lt)

            @p.on("tensor")
            def f(e, vv_lt=vv_lt, fr=pg[3]):
                W(e, "vv", vv_lt)
                WT(e, fr)
                for t4 in range(4):
                    e.transpose(pW[3][0:128, t4 * 8:t4 * 8 + 8],
                                ltmp[:, t4 * 128:(t4 + 1) * 128], ident[0:8, 0:8])
                e.drain().then_inc(SEM["tv"], 1)
            tv_lt = p.inc("tv")

            @p.on("vector")
            def f(e, grp=grp, tv_lt=tv_lt, sx_rbB=sx_rbB):
                W(e, "tv", tv_lt)
                W(e, "sx", sx_rbB)
                for t4 in range(4):
                    col = (grp * 4 + t4) * 8
                    e.tensor_tensor(logit_s[:, col:col + 8],
                                    pW[3][0:128, t4 * 8:t4 * 8 + 8], rbadjB[:],
                                    op=ALU.add)
                e.drain().then_inc(SEM["vv"], 1)
            pg[3] = ("vv", p.inc("vv"))
            take(1)

            for dg in range(4):
                @p.on("gpsimd")
                def f(e, dg=dg, g=("tv", p.cnt["tv"])):
                    WT(e, g)
                    e.dma_start(out=cwb.rearrange("p (c f) -> p c f", c=DC),
                                in_=core_w[:, dg * 512:(dg + 1) * 512]
                                .rearrange("(c p) f -> p c f", p=128))\
                     .then_inc(SEM["w20"], 16)
                cw_t = p.inc("w20", 16)

                for tcl in range(4):
                    pw = tcl % 2
                    @p.on("tensor")
                    def f(e, tcl=tcl, pw=pw, cw_t=cw_t, g2d=g2_done, fr=pg[pw]):
                        if tcl == 0:
                            W(e, "w20", cw_t)
                            W(e, "sc", g2d)
                        WT(e, fr)
                        for kc in range(DC):
                            e.matmul(pW[pw][0:128, 0:512],
                                     g2[:, kc * 512 + tcl * 128:kc * 512 + tcl * 128 + 128],
                                     cwb[:, kc * 512:(kc + 1) * 512],
                                     start=(kc == 0), stop=(kc == DC - 1))
                        e.drain().then_inc(SEM["tv"], 1)
                    tv_c = p.inc("tv")

                    @p.on("vector")
                    def f(e, tv_c=tv_c, pw=pw):
                        W(e, "tv", tv_c)
                        e.tensor_copy(ctmp[:], pW[pw][0:128, 0:512])
                        e.drain().then_inc(SEM["vv"], 1)
                    vv_ct = p.inc("vv")
                    pg[pw] = ("vv", vv_ct)
                    take(1)

                    @p.on("sync")
                    def f(e, blk=blk, grp=grp, tcl=tcl, dg=dg, vv_ct=vv_ct):
                        W(e, "vv", vv_ct)
                        tok = blk * 1024 + grp * 512 + tcl * 128
                        e.dma_start(out=scr_core[tok:tok + 128, dg * 512:(dg + 1) * 512],
                                    in_=ctmp[:]).then_inc(SEM["qs0"], 16)
                    qst = p.inc("qs0", 16)

                    @p.on("vector")
                    def f(e, qst=qst):
                        W(e, "qs0", qst)
                        e.drain().then_inc(SEM["vv"], 1)
                    p.inc("vv")

        for t8 in range(8):
            @p.on("vector")
            def f(e, t8=t8):
                e.reduce_max(rm[:], logit_s[:, t8 * 8:(t8 + 1) * 8], axis=AX.X)
                e.drain()
                e.tensor_scalar(rm[:], rm[:], -1.0, None, op0=ALU.mult)
                e.drain().then_inc(SEM["vv"], 1)
            vv_m = p.inc("vv")

            @p.on("scalar")
            def f(e, t8=t8, vv_m=vv_m):
                W(e, "vv", vv_m)
                e.activation(rex[:], logit_s[:, t8 * 8:(t8 + 1) * 8], AF.Exp,
                             bias=rm[:], accum_out=rz_[:]).then_inc(SEM["sc"], 1)
            sc_e = p.inc("sc")

            @p.on("vector")
            def f(e, sc_e=sc_e):
                W(e, "sc", sc_e)
                e.reciprocal(rm[:], rz_[:])
                e.drain()
                e.tensor_scalar(rex[:], rex[:], rm[:], None, op0=ALU.mult)
                e.drain()
                e.max(srt[:], rex[:])
                e.drain()
                e.tensor_tensor(rg1[:], srt[:, 1:2], srt[:, 0:1], op=ALU.subtract)
                e.drain().then_inc(SEM["vv"], 1)
            vv_p = p.inc("vv")

            @p.on("scalar")
            def f(e, vv_p=vv_p):
                W(e, "vv", vv_p)
                e.activation(rg1[:], rg1[:], AF.Exp).then_inc(SEM["sc"], 1)
            sc_g = p.inc("sc")

            @p.on("vector")
            def f(e, t8=t8, sc_g=sc_g):
                W(e, "sc", sc_g)
                e.tensor_scalar(rg0[:], rg1[:], 1.0, None, op0=ALU.add)
                e.drain()
                e.reciprocal(rg0[:], rg0[:])
                e.drain()
                e.tensor_tensor(rg1[:], rg1[:], rg0[:], op=ALU.mult)
                e.drain()
                e.tensor_scalar(rt0[:], rex[:], srt[:, 0:1], rg0[:],
                                op0=ALU.is_equal, op1=ALU.mult)
                e.drain()
                e.tensor_scalar(w10s[:, 0:8], rex[:], srt[:, 1:2], rg1[:],
                                op0=ALU.is_equal, op1=ALU.mult)
                e.drain()
                e.tensor_tensor(w10s[:, 0:8], w10s[:, 0:8], rt0[:], op=ALU.add)
                e.tensor_tensor(w10s[:, 8:9], rg0[:], rg1[:], op=ALU.add)
                e.memset(w10s[:, 9:10], 1.0)
                e.drain().then_inc(SEM["vv"], 1)
            vv_w = p.inc("vv")

            @p.on("tensor")
            def f(e, vv_w=vv_w, fr=pg[3]):
                W(e, "vv", vv_w)
                WT(e, fr)
                e.transpose(pW[3][0:10, 0:128], w10s[:], ident[:])
                e.drain().then_inc(SEM["tv"], 1)
            tv_w10 = p.inc("tv")

            @p.on("vector")
            def f(e, t8=t8, tv_w10=tv_w10):
                W(e, "tv", tv_w10)
                e.tensor_copy(w10fm[:, t8 * 128:(t8 + 1) * 128], pW[3][0:10, 0:128])
                e.drain().then_inc(SEM["vv"], 1)
            pg[3] = ("vv", p.inc("vv"))
        w10_done = p.cnt["vv"]

        @p.on("gpsimd")
        def f(e, w10_done=w10_done):
            W(e, "vv", w10_done)
            for r in range(9):
                e.dma_start(out=scr_rows[6 + r:7 + r, 0:1024],
                            in_=w10fm[r:r + 1, :]).then_inc(SEM["sg"], 16)
        sx_w10 = p.inc("sg", 16 * 9)

        take(len(p5q))
        if blk in (0, 2):
            @p.on("sync")
            def f(e, s_=s_, g=p.cnt["vv"], gs=p.cnt["sx"]):
                W(e, "vv", g)
                W(e, "sx", gs)
                e.dma_start(out=bcastA[:],
                            in_=scr_rows[s_ * 2 + 1, 0:2048].unsqueeze(0)
                            .partition_broadcast(128)).then_inc(SEM["sx"], 16)
            SXGB[s_] = p.inc("sx", 16)
        w1g[0] = w1g[1] = ("tv", p.cnt["tv"])
        w2g[0] = w2g[1] = ("tv", p.cnt["tv"])
        for ei in range(9):
            shared = (ei == 8)
            h_e = L if shared else EHID[ei]
            hch = h_e // 128
            w1_src = shared_w1 if shared else ew1[ei]
            w2_src = shared_w2 if shared else ew2[ei]
            scale = 0.1 if shared else 1.0
            hid_free = ("tv", p.cnt["tv"])

            b1_t = None
            we_done = None
            if not shared:
                @p.on("gpsimd")
                def f(e, ei=ei, hch=hch, g=("sc", p.cnt["sc"]),
                      pcw=(PC_TOTAL if blk == 0 and ei == 0 else 0)):
                    WT(e, g)
                    if pcw:
                        W(e, "pc", pcw)
                    e.dma_start(out=b1f[:, 0:hch],
                                in_=eb1[ei].rearrange("(c p) -> p c", p=128),
                                allow_slow_non_contiguous=True).then_inc(SEM["sg"], 16)
                b1_t = p.inc("sg", 16)

                @p.on("gpsimd")
                def f(e, ei=ei, sx_w10=sx_w10, g=p.cnt["vv"], gs=p.cnt["sg"]):
                    W(e, "sg", max(sx_w10, gs))
                    W(e, "vv", g)
                    e.dma_start(out=webb[:], in_=scr_rows[6 + ei, 0:1024].unsqueeze(0)
                                .partition_broadcast(128)).then_inc(SEM["sg"], 16)
                we_done_sg = p.inc("sg", 16)
                we_done = None

            gb1_done = None
            if blk == 0 and not shared:
                @p.on("scalar")
                def f(e, hch=hch, b1_t=b1_t, gt=p.cnt["tv"]):
                    W(e, "sg", b1_t)
                    W(e, "tv", gt)
                    e.activation(gb1sb[:, 0:hch], b1f[:, 0:hch], AF.Gelu)\
                     .then_inc(SEM["sc"], 1)
                gb1_done = p.inc("sc")

            for hs in range(h_e // 512):
                sl1 = hs % 2
                ssem1 = "ws1" if sl1 == 0 else "w11"
                @p.on("gpsimd")
                def f(e, hs=hs, sl1=sl1, ssem1=ssem1, w1_src=w1_src,
                      g=w1g[sl1]):
                    WT(e, g)
                    e.dma_start(out=w1s[sl1].rearrange("p (c f) -> p c f", c=8),
                                in_=w1_src[:, hs * 512:(hs + 1) * 512]
                                .rearrange("(c p) f -> p c f", p=128))\
                     .then_inc(SEM[ssem1], 16)
                w1_t = p.inc(ssem1, 16)

                for hcl in range(4):
                    hc = hs * 4 + hcl
                    pz = hc % 2
                    @p.on("tensor")
                    def f(e, hcl=hcl, pz=pz, sl1=sl1, ssem1=ssem1, w1_t=w1_t,
                          fr=zg[pz], hf=hid_free, gg=("sc", p.cnt["sc"])):
                        if hcl == 0:
                            W(e, ssem1, w1_t)
                            WT(e, gg)
                        WT(e, fr, hf)
                        for half in range(2):
                            for kc in range(LCH):
                                e.matmul(pZ[pz][0:128, half * 512:half * 512 + 512],
                                         w1s[sl1][:, kc * 512 + hcl * 128:kc * 512 + hcl * 128 + 128],
                                         gbuf[:, kc * 1024 + half * 512:kc * 1024 + half * 512 + 512],
                                         start=(kc == 0), stop=(kc == LCH - 1))
                        e.drain().then_inc(SEM["tv"], 1)
                    tv_z = p.inc("tv")
                    if hcl == 3:
                        w1g[sl1] = ("tv", tv_z)

                    @p.on("scalar")
                    def f(e, hc=hc, pz=pz, tv_z=tv_z, shared=shared):
                        W(e, "tv", tv_z)
                        bias = (sb1fm[:, hc:hc + 1] if shared else b1f[:, hc:hc + 1])
                        e.activation(hid[:, hc * 1024:(hc + 1) * 1024],
                                     pZ[pz][0:128, 0:1024], AF.Gelu,
                                     bias=bias).then_inc(SEM["sc"], 1)
                    sc_h = p.inc("sc")
                    zg[pz] = ("sc", sc_h)

                    if not shared:
                        @p.on("vector")
                        def f(e, hc=hc, sc_h=sc_h, we_done=we_done_sg):
                            W(e, "sc", sc_h)
                            W(e, "sg", we_done)
                            e.tensor_tensor(hid[:, hc * 1024:(hc + 1) * 1024],
                                            hid[:, hc * 1024:(hc + 1) * 1024],
                                            webb[:], op=ALU.mult)
                            e.drain().then_inc(SEM["vv"], 1)
                        p.inc("vv")
            hid_done_v = p.cnt["vv"]
            hid_done_s = p.cnt["sc"]

            for lc in range(LCH):
                sl2 = lc % 2
                ssem2 = "w21" if sl2 == 0 else "w22"
                @p.on("gpsimd")
                def f(e, lc=lc, sl2=sl2, ssem2=ssem2, w2_src=w2_src, hch=hch,
                      g=w2g[sl2]):
                    WT(e, g)
                    e.dma_start(out=w2s[sl2][:, 0:hch * 128]
                                .rearrange("p (c f) -> p c f", f=128),
                                in_=w2_src[:, lc * 128:(lc + 1) * 128]
                                .rearrange("(c p) f -> p c f", p=128))\
                     .then_inc(SEM[ssem2], 16)
                w2_t = p.inc(ssem2, 16)

                for half in range(2):
                    pw = half % 2
                    @p.on("tensor")
                    def f(e, half=half, pw=pw, sl2=sl2, ssem2=ssem2, w2_t=w2_t,
                          hch=hch, fr=pg[pw], hv=hid_done_v, hsd=hid_done_s):
                        if half == 0:
                            W(e, ssem2, w2_t)
                            W(e, "vv", hv)
                            W(e, "sc", hsd)
                        WT(e, fr)
                        for hc in range(hch):
                            e.matmul(pW[pw][0:128, 0:512],
                                     w2s[sl2][:, hc * 128:(hc + 1) * 128],
                                     hid[:, hc * 1024 + half * 512:hc * 1024 + half * 512 + 512],
                                     start=(hc == 0), stop=(hc == hch - 1))
                        e.drain().then_inc(SEM["tv"], 1)
                    tv_w2 = p.inc("tv")
                    if half == 1:
                        w2g[sl2] = ("tv", tv_w2)

                    @p.on("vector")
                    def f(e, lc=lc, half=half, pw=pw, ei=ei, scale=scale, tv_w2=tv_w2):
                        W(e, "tv", tv_w2)
                        dst = moe[:, lc * 1024 + half * 512:lc * 1024 + half * 512 + 512]
                        if ei == 0:
                            e.tensor_copy(dst, pW[pw][0:128, 0:512])
                        else:
                            e.scalar_tensor_tensor(dst, pW[pw][0:128, 0:512], scale,
                                                   dst, op0=ALU.mult, op1=ALU.add)
                        e.drain().then_inc(SEM["vv"], 1)
                    pg[pw] = ("vv", p.inc("vv"))

                if blk == 0 and not shared:
                    @p.on("tensor")
                    def f(e, hch=hch, sl2=sl2, gb1_done=gb1_done, fr=pg[2]):
                        W(e, "sc", gb1_done)
                        WT(e, fr)
                        for hc in range(hch):
                            e.matmul(pW[2][0:128, 0:1],
                                     w2s[sl2][:, hc * 128:(hc + 1) * 128],
                                     gb1sb[:, hc:hc + 1],
                                     start=(hc == 0), stop=(hc == hch - 1))
                        e.drain().then_inc(SEM["tv"], 1)
                    tv_cp = p.inc("tv")
                    w2g[sl2] = ("tv", tv_cp)

                    @p.on("vector")
                    def f(e, tv_cp=tv_cp):
                        W(e, "tv", tv_cp)
                        e.tensor_copy(ctmp[:, 0:1], pW[2][0:128, 0:1])
                        e.drain().then_inc(SEM["vv"], 1)
                    vv_cp = p.inc("vv")
                    pg[2] = ("vv", vv_cp)

                    @p.on("sync")
                    def f(e, ei=ei, lc=lc, vv_cp=vv_cp):
                        W(e, "vv", vv_cp)
                        e.dma_start(out=scr_c10[ei:ei + 1, lc * 128:(lc + 1) * 128],
                                    in_=ctmp[:, 0:1]).then_inc(SEM["qs0"], 16)
                    qcp = p.inc("qs0", 16)

                    @p.on("vector")
                    def f(e, qcp=qcp):
                        W(e, "qs0", qcp)
                        e.drain().then_inc(SEM["vv"], 1)
                    p.inc("vv")

        if blk == 0:
            @p.on("sync")
            def f(e, g=p.cnt["qs0"], gt=p.cnt["tv"], gs=p.cnt["sx"]):
                W(e, "qs0", g)
                W(e, "tv", gt)
                W(e, "sx", gs)
                e.dma_start(out=ct8, in_=scr_c10[0:8, :]).then_inc(SEM["sx"], 16)
            sx_c8 = p.inc("sx", 16)

            @p.on("vector")
            def f(e, sx_c8=sx_c8):
                W(e, "sx", sx_c8)
                e.tensor_scalar(c10sb[0:8, :], ct8, -1.0, None, op0=ALU.mult)
                e.drain()
                e.tensor_tensor(ct8, ct8, b2s[:], op=ALU.add)
                e.drain().then_inc(SEM["vv"], 1)
            vv_c8 = p.inc("vv")

            @p.on("tensor")
            def f(e, vv_c8=vv_c8, f2=pg[2], f3=pg[3]):
                W(e, "vv", vv_c8)
                WT(e, f2, f3)
                e.matmul(pW[3][0:1, 0:512], one8[:], ct8[:, 0:512], start=True, stop=True)
                e.matmul(pW[2][0:1, 0:512], one8[:], ct8[:, 512:1024], start=True, stop=True)
                e.drain().then_inc(SEM["tv"], 1)
            tv_ct = p.inc("tv")

            @p.on("vector")
            def f(e, tv_ct=tv_ct):
                W(e, "tv", tv_ct)
                e.tensor_copy(ctrow[:, 0:512], pW[3][0:1, 0:512])
                e.tensor_copy(ctrow[:, 512:1024], pW[2][0:1, 0:512])
                e.drain().then_inc(SEM["vv"], 1)
            vv_ctr = p.inc("vv")
            pg[2] = pg[3] = ("vv", vv_ctr)

            @p.on("gpsimd")
            def f(e, vv_ctr=vv_ctr):
                W(e, "vv", vv_ctr)
                e.dma_start(out=c10sb[8:9, :], in_=ctrow).then_inc(SEM["sg"], 16)
            sg_c = p.inc("sg", 16)

            @p.on("vector")
            def f(e, sg_c=sg_c):
                W(e, "sg", sg_c)
                e.tensor_scalar(ctrow, sb2r[:], 0.1, None, op0=ALU.mult)
                e.drain().then_inc(SEM["vv"], 1)
            vv_s2 = p.inc("vv")

            @p.on("gpsimd")
            def f(e, vv_s2=vv_s2):
                W(e, "vv", vv_s2)
                e.dma_start(out=c10sb[9:10, :], in_=ctrow).then_inc(SEM["sg"], 16)
            sg_c2 = p.inc("sg", 16)
            c10_ready = ("sg", sg_c2)

        for lc in range(LCH):
            for hf in range(2):
                pw = (lc * 2 + hf) % 2
                @p.on("tensor")
                def f(e, lc=lc, hf=hf, pw=pw, fr=pg[pw], cg=c10_ready,
                      wg=("vv", w10_done)):
                    WT(e, fr, cg, wg)
                    e.matmul(pW[pw][0:128, 0:512],
                             c10sb[0:10, lc * 128:(lc + 1) * 128],
                             w10fm[0:10, hf * 512:(hf + 1) * 512],
                             start=True, stop=True)
                    e.drain().then_inc(SEM["tv"], 1)
                tv_cx = p.inc("tv")

                @p.on("vector")
                def f(e, lc=lc, hf=hf, pw=pw, tv_cx=tv_cx):
                    W(e, "tv", tv_cx)
                    a = lc * 1024 + hf * 512
                    e.scalar_tensor_tensor(moebf[:, a:a + 512],
                                           pW[pw][0:128, 0:512], 1.0,
                                           moe[:, a:a + 512],
                                           op0=ALU.mult, op1=ALU.add)
                    e.drain().then_inc(SEM["vv"], 1)
                pg[pw] = ("vv", p.inc("vv"))

        moebf_done = p.cnt["vv"]
        steps = []
        upt_box = {}

        for dg in range(4):
            us = dg % 2

            def s_dma(dg=dg, us=us):
                @p.on("gpsimd")
                def f(e, dg=dg, us=us, pin=p.cnt["up0"],
                      g=("tv", p5upt[us] if dg >= 2 else p.cnt["tv"])):
                    WT(e, g)
                    W(e, "up0", pin)
                    e.dma_start(out=ups2[us].rearrange("p (c f) -> p c f", c=LCH),
                                in_=up_w[:, dg * 512:(dg + 1) * 512]
                                .rearrange("(c p) f -> p c f", p=128))\
                     .then_inc(SEM["up0"], 16)
                upt_box[dg] = p.inc("up0", 16)
            steps.append(s_dma)

            for tc in range(8):
                def s_tc(dg=dg, us=us, tc=tc, blk=blk, s_=s_, moebf=moebf,
                         moebf_done=moebf_done):
                    tok = blk * 1024 + tc * 128
                    sl = tc % 2
                    xsl = "cs0" if sl == 0 else "cs1"

                    @p.on("sync")
                    def f(e, tok=tok, sl=sl, dg=dg, xsl=xsl, g=p5xf[sl]):
                        if g:
                            W(e, "vv", g)
                        e.dma_start(out=xms[sl],
                                    in_=scr_xmid[tok:tok + 128,
                                                 dg * 512:(dg + 1) * 512])\
                         .then_inc(SEM[xsl], 16)
                        e.dma_start(out=cbs[sl],
                                    in_=scr_core[tok:tok + 128,
                                                 dg * 512:(dg + 1) * 512])\
                         .then_inc(SEM[xsl], 16)
                    xtg = p.inc(xsl, 32)

                    pw = tc % 2
                    @p.on("tensor")
                    def f(e, tc=tc, dg=dg, us=us, pw=pw, fr=pg[pw],
                          md=moebf_done, up_t=upt_box[dg], moebf=moebf):
                        if tc == 0:
                            W(e, "up0", up_t)
                            W(e, "vv", md)
                        WT(e, fr)
                        for lc in range(LCH):
                            e.matmul(pW[pw][0:128, 0:512],
                                     moebf[:, lc * 1024 + tc * 128:lc * 1024 + tc * 128 + 128],
                                     ups2[us][:, lc * 512:(lc + 1) * 512],
                                     start=(lc == 0), stop=False)
                        e.matmul(pW[pw][0:128, 0:512], onesb[0:1, 0:128],
                                 ubcbr[0:1, dg * 512:(dg + 1) * 512],
                                 start=False, stop=True)
                        if tc == 7:
                            e._e.drain().then_inc(SEM["tv"], 1)
                        else:
                            e.drain().then_inc(SEM["tv"], 1)
                    tv_u = p.inc("tv")
                    if tc == 7:
                        p5upt[us] = tv_u
                        if dg == 3:
                            GBF[blk % 2] = tv_u

                    @p.on("vector")
                    def f(e, sl=sl, dg=dg, pw=pw, tv_u=tv_u, xsl=xsl, xtg=xtg,
                          sx_gb=SXGB[s_], fso=p5fso[sl],
                          ssem="so" if sl == 0 else "so2"):
                        W(e, "tv", tv_u)
                        W(e, xsl, xtg)
                        W(e, "sx", sx_gb)
                        if fso:
                            W(e, ssem, fso)
                        e.tensor_tensor(fins[sl], pW[pw][0:128, 0:512],
                                        cbs[sl], op=ALU.add)
                        e.drain()
                        e.tensor_tensor(fins[sl], fins[sl],
                                        bcastA[:, dg * 512:(dg + 1) * 512],
                                        op=ALU.mult)
                        e.drain()
                        e.tensor_tensor(fins[sl], xms[sl], fins[sl], op=ALU.add)
                        e.drain().then_inc(SEM["vv"], 1)
                    vv_f = p.inc("vv")
                    pg[pw] = ("vv", vv_f)
                    p5xf[sl] = vv_f

                    @p.on("sync")
                    def f(e, s_=s_, tok=tok, dg=dg, sl=sl, vv_f=vv_f):
                        W(e, "vv", vv_f)
                        e.dma_start(out=out[s_, tok - s_ * S:tok - s_ * S + 128,
                                            dg * 512:(dg + 1) * 512],
                                    in_=fins[sl])\
                         .then_inc(SEM["so" if sl == 0 else "so2"], 16)
                    p5fso[sl] = p.inc("so" if sl == 0 else "so2", 16)
                steps.append(s_tc)

        return c10_ready, steps

    c10_ready = None
    p5q = []
    for blk in range(4):
        c10_ready, p5q = emit_blk(blk, c10_ready, p5q)
    for f_ in p5q:
        f_()

    @p.on("sync")
    def f(e, w=p.cnt["so"], w2=p.cnt["so2"]):
        if w:
            e.wait_ge(SEM["so"], w)
        if w2:
            e.wait_ge(SEM["so2"], w2)

    st.close()
    pp.close()


def build(debug=(), phases=("p0", "p1", "p2", "p345")):
    dbg = set(debug)
    PH = set(phases)
    nc = bass.Bass("TRN2", target_bir_lowering=False)
    ctx = ExitStack()
    p = P(nc, ctx)
    SEMN = ("sx", "sxa", "sg", "tv", "vv", "sc", "gc", "so", "so2", "pc", "qs0",
            "xs0", "xs1", "cs0", "cs1", "ws0", "ws1", "w20", "w21", "w11", "w22", "up0")
    for s_ in SEMN:
        p.sem(s_)
    SEM = p.sems

    LATE = {}
    def W(e, semname, val):
        if callable(val):
            val = val()
        if val > 0:
            e.wait_ge(SEM[semname], val)

    def WG(e, buf):
        sn, v = p.guard(buf)
        if sn is not None and v > 0:
            e.wait_ge(SEM[sn], v)

    def barrier():
        tgt = {k: p.cnt[k] for k in SEMN}
        for eng in P.ENGS:
            @p.on(eng)
            def f(e, tgt=dict(tgt)):
                for k, v in tgt.items():
                    if v:
                        e.wait_ge(SEM[k], v)

    din = {}
    def I(name, shape):
        din[name] = nc.dram_tensor(name, list(shape), F32, kind="ExternalInput").ap()
        return din[name]

    x_in = I("x", (NSEQ, S, D)); c_in = I("c", (NSEQ, D))
    wqkv = I("Wqkv", (D, QKV)); bqkv = I("bqkv", (QKV,))
    wout_w = I("Wout", (D, D)); bout = I("bout", (D,))
    down_w = I("down_w", (D, L)); down_b = I("down_b", (L,))
    up_w = I("up_w", (L, D)); up_b = I("up_b", (D,))
    router_w = I("router_w", (D, 8)); router_b = I("router_b", (8,))
    shared_w1 = I("shared_w1", (L, L)); shared_b1 = I("shared_b1", (L,))
    shared_w2 = I("shared_w2", (L, L)); shared_b2 = I("shared_b2", (L,))
    core_w = I("core_w", (D, D)); core_b = I("core_b", (D,))
    e_w1 = [I(f"e{e}_w1", (L, EHID[e])) for e in range(8)]
    e_b1 = [I(f"e{e}_b1", (EHID[e],)) for e in range(8)]
    e_w2 = [I(f"e{e}_w2", (EHID[e], L)) for e in range(8)]
    e_b2 = [I(f"e{e}_b2", (L,)) for e in range(8)]
    wmod = I("wmod", (D, 6 * D))
    ident_in = I("ident_f32", (128, 128))
    perm_in = I("rope_perm", (128, 128))
    masks_in = I("causal_masks", (128, 2048))
    nsin_in = I("n_sin", (16, S)); ncos_in = I("n_cos", (16, S))
    rcost_in = I("router_cost", (1, 8))
    selc_in = I("rope_selc", (16, 128))
    sel1_in = I("rope_sel1", (1, 128))
    sels_in = I("rope_sels", (16, 128))

    out = nc.dram_tensor("out", [NSEQ, S, D], F32, kind="ExternalOutput").ap()
    dbg_t = {}
    def DBG(name, shape, dtype=F32):
        if name in dbg:
            dbg_t[name] = nc.dram_tensor("dbg_" + name, list(shape), dtype,
                                         kind="ExternalOutput").ap()
        return dbg_t.get(name)

    scr_q = nc.dram_tensor("scr_q", [64, NH * S], BF16, kind="Internal").ap()
    scr_xmid = nc.dram_tensor("scr_xmid", [T, D], F32, kind="Internal").ap()
    scr_core = nc.dram_tensor("scr_core", [T, D], F32, kind="Internal").ap()
    scr_rows = nc.dram_tensor("scr_rows", [16, 4096], F32, kind="Internal").ap()
    scr_c10 = nc.dram_tensor("scr_c10", [16, L], F32, kind="Internal").ap()

    scr_e1b = [nc.dram_tensor(f"scr_e1b{e}", [L, EHID[e]], BF16,
                              kind="Internal").ap() for e in range(8)]
    scr_e2b = [nc.dram_tensor(f"scr_e2b{e}", [EHID[e], L], BF16,
                              kind="Internal").ap() for e in range(8)]
    scr_s1b = nc.dram_tensor("scr_s1b", [L, L], BF16, kind="Internal").ap()
    scr_s2b = nc.dram_tensor("scr_s2b", [L, L], BF16, kind="Internal").ap()

    PRECAST = []

    def _pc(dst, src, n):
        for c in range(4):
            a, b = c * n // 4, (c + 1) * n // 4
            def f(e, dst=dst[a:b, :], src=src[a:b, :]):
                e.dma_start(out=dst, in_=src).then_inc(SEM["pc"], 16)
            PRECAST.append(f)

    for e_ in range(8):
        _pc(scr_e1b[e_], e_w1[e_], L)
        _pc(scr_e2b[e_], e_w2[e_], EHID[e_])
    _pc(scr_s1b, shared_w1, L)
    _pc(scr_s2b, shared_w2, L)
    PC_TOTAL = 16 * len(PRECAST)

    def emit_pc(n):
        for _ in range(min(n, len(PRECAST))):
            p.q["gpsimd"].append(PRECAST.pop(0))
            p.inc("pc", 16)

    sbp = ExitStack()
    def SB(name, shape, dtype, st=sbp):
        return st.enter_context(nc.sbuf_tensor(name, list(shape), dtype))

    ident = SB("ident", (128, 128), F32)
    onesb = SB("onesb", (1, 512), BF16)
    cpool = SB("cpool", (128, 4), F32)
    modv = SB("modv", (128, NSEQ * 4 * DC), F32)
    bqkv_fm = SB("bqkv_fm", (128, 16), F32)
    bkfm = SB("bkfm", (64, 8), F32)
    dbfm = SB("dbfm", (128, LCH), F32)
    sb1fm = SB("sb1fm", (128, LCH), F32)
    ln2ab = SB("ln2ab", (128, 2 * (T // 128)), F32)

    ubcbr = SB("ubcbr", (1, D), BF16)
    bcastA = SB("bcastA", (128, 2048), F32)
    rbadj = SB("rbadj", (1, 8), F32)
    rwsb = SB("rwsb", (128, DC * 8), BF16)
    gb1sb = SB("gb1sb", (128, 32), BF16)
    b2s = SB("b2s", (8, L), F32)
    sb2r = SB("sb2r", (1, L), F32)
    sb12 = ExitStack()
    masksb = SB("masksb", (128, 2048), BF16, st=sb12)
    ropeC = SB("ropeC", (128, S), BF16, st=sb12)
    ropeS = SB("ropeS", (128, S), BF16, st=sb12)
    identb = SB("identb", (128, 128), BF16, st=sb12)
    permb = SB("permb", (128, 128), BF16, st=sb12)
    onesf = SB("onesf", (1, 512), F32, st=sb12)
    bvB = SB("bvB", (128, 512), F32, st=sb12)
    boutr = SB("boutr", (1, D), BF16, st=sb12)

    nc.const_aps.register(F32, EPS, cpool[:, 0:1])

    @p.on("sync")
    def f(e):
        e.dma_start(out=ident[:], in_=ident_in).then_inc(SEM["sxa"], 16)
        e.dma_start(out=bvB[:], in_=bqkv[2560:3072].unsqueeze(0)
                    .partition_broadcast(128)).then_inc(SEM["sxa"], 16)
        e.dma_start(out=rbadj[:], in_=router_b.unsqueeze(0)).then_inc(SEM["sxa"], 16)
        e.dma_start(out=bqkv_fm[:], in_=bqkv[0:2048].rearrange("(c p) -> p c", p=128),
                    allow_slow_non_contiguous=True).then_inc(SEM["sxa"], 16)
        e.dma_start(out=bkfm[:], in_=bqkv[2048:2560].rearrange("(h k) -> k h", k=64),
                    allow_slow_non_contiguous=True).then_inc(SEM["sxa"], 16)
        e.dma_start(out=dbfm[:], in_=down_b.rearrange("(c p) -> p c", p=128),
                    allow_slow_non_contiguous=True).then_inc(SEM["sxa"], 16)
        e.dma_start(out=sb1fm[:], in_=shared_b1.rearrange("(c p) -> p c", p=128),
                    allow_slow_non_contiguous=True).then_inc(SEM["sxa"], 16)
        for ee in range(8):
            e.dma_start(out=b2s[ee:ee + 1, :], in_=e_b2[ee].unsqueeze(0)).then_inc(SEM["sxa"], 16)
        e.dma_start(out=sb2r[:], in_=shared_b2.unsqueeze(0)).then_inc(SEM["sxa"], 16)
        e.dma_start(out=selc_s[:], in_=selc_in).then_inc(SEM["sxa"], 16)
        e.dma_start(out=sel1_s[:], in_=sel1_in).then_inc(SEM["sxa"], 16)
        e.dma_start(out=sels_s[:], in_=sels_in).then_inc(SEM["sxa"], 16)
    sx_base_unused = p.inc("sxa", 16 * 19)

    @p.on("gpsimd")
    def f(e):
        e.dma_start(out=identb[:], in_=ident_in).then_inc(SEM["sg"], 16)
        e.dma_start(out=permb[:], in_=perm_in).then_inc(SEM["sg"], 16)
        e.dma_start(out=boutr[:], in_=bout.unsqueeze(0)).then_inc(SEM["sg"], 16)
        e.dma_start(out=masksb[:], in_=masks_in).then_inc(SEM["sg"], 16)
        e.dma_start(out=rwsb[:].rearrange("p (c e) -> p c e", e=8),
                    in_=router_w.rearrange("(c p) e -> p c e", p=128),
                    allow_slow_non_contiguous=True).then_inc(SEM["sg"], 16)
    sg_base = p.inc("sg", 80)

    p0 = ExitStack()
    selc_s = p0.enter_context(nc.sbuf_tensor("selc_s", [16, 128], F32))
    sel1_s = p0.enter_context(nc.sbuf_tensor("sel1_s", [1, 128], F32))
    sels_s = p0.enter_context(nc.sbuf_tensor("sels_s", [16, 128], F32))
    p0a = p0.enter_context(nc.sbuf_tensor("p0a", [16, S], F32))
    p0b = p0.enter_context(nc.sbuf_tensor("p0b", [16, S], F32))
    p0n = p0.enter_context(nc.sbuf_tensor("p0n", [16, S], F32))
    p0f = p0.enter_context(nc.sbuf_tensor("p0f", [16, 1], F32))
    p0sin = p0.enter_context(nc.sbuf_tensor("p0sin", [16, S], F32))
    ubtmp = p0.enter_context(nc.sbuf_tensor("ubtmp", [1, 2 * D], F32))
    rct = p0.enter_context(nc.sbuf_tensor("rct", [1, 8], F32))
    ctm = p0.enter_context(nc.sbuf_tensor("ctm", [NSEQ, D], F32))
    scfm = p0.enter_context(nc.sbuf_tensor("scfm", [128, 2 * DC], BF16))
    scfm32 = p0.enter_context(nc.sbuf_tensor("scfm32", [128, 2 * DC], F32))
    wt = [p0.enter_context(nc.sbuf_tensor(f"wt{i}", [128, DC * 512], BF16))
          for i in range(2)]
    wtF = [p0.enter_context(nc.sbuf_tensor(f"wtF{i}", [128, DC * 512], F32))
           for i in range(2)]
    mtmp = [p0.enter_context(nc.sbuf_tensor(f"mtmp{i}", [2, 512], F32))
            for i in range(2)]
    pmod2 = [p0.enter_context(nc.psum_tensor(f"pmod2{i}", [2, 512], F32))
             for i in range(2)]
    modsc = p0.enter_context(nc.sbuf_tensor("modsc", [128, 8], F32))
    modg = p0.enter_context(nc.sbuf_tensor("modg", [128, 8], F32))
    pmod = p0.enter_context(nc.psum_tensor("pmod", [128, 512], F32))
    ptr0 = p0.enter_context(nc.psum_tensor("ptr0", [128, 512], F32))

    @p.on("vector")
    def f(e):
        e.memset(cpool[:, 0:1], EPS)
        e.memset(onesb[:], 1.0)
        e.memset(onesf[:], 1.0)
        e.drain()
        e.drain().then_inc(SEM["vv"], 1)
    p.inc("vv")

    @p.on("gpsimd")
    def f(e):
        e.iota(p0a[:], pattern=[[1, S]], base=0, channel_multiplier=0,
               allow_small_or_imprecise_dtypes=True)
        e.iota(p0f[:], pattern=[[0, 1]], base=0, channel_multiplier=1,
               allow_small_or_imprecise_dtypes=True).then_inc(SEM["gc"], 1)
    gc_iota = p.inc("gc")

    @p.on("scalar")
    def f(e):
        W(e, "gc", gc_iota)
        e.activation(p0f[:], p0f[:], AF.Exp, scale=NEG_LN_BASE_16).then_inc(SEM["sc"], 1)
    sc_if = p.inc("sc")

    @p.on("sync")
    def f(e):
        e.dma_start(out=p0n[:], in_=nsin_in).then_inc(SEM["sxa"], 16)
        e.dma_start(out=ubtmp[:, 0:D], in_=up_b.unsqueeze(0)).then_inc(SEM["sxa"], 16)
        e.dma_start(out=ubtmp[:, D:2 * D], in_=core_b.unsqueeze(0)).then_inc(SEM["sxa"], 16)
        e.dma_start(out=rct[:], in_=rcost_in).then_inc(SEM["sxa"], 16)
        e.dma_start(out=ctm[:], in_=c_in).then_inc(SEM["sxa"], 16)
    sx_p0_unused = p.inc("sxa", 16 * 5)

    @p.on("vector")
    def f(e):
        W(e, "sc", sc_if)
        W(e, "sxa", lambda: LATE["sxa"])
        e.tensor_scalar(p0b[:], p0a[:], p0f[:], None, op0=ALU.mult)
        e.drain()
        e.scalar_tensor_tensor(p0sin[:], p0n[:], -TWO_PI, p0b[:], op0=ALU.mult, op1=ALU.add)
        e.drain()
        e.tensor_scalar(p0sin[:], p0sin[:], PI, -PI, op0=ALU.min, op1=ALU.max)
        e.drain()
        e.tensor_tensor(ubcbr[:], ubtmp[:, 0:D], ubtmp[:, D:2 * D], op=ALU.add)
        e.scalar_tensor_tensor(rbadj[:], rct[:], -COST_LAMBDA, rbadj[:],
                               op0=ALU.mult, op1=ALU.add)
        e.tensor_scalar(bkfm[:], bkfm[:], 0.125, None, op0=ALU.mult)
        e.drain()
        e.drain().then_inc(SEM["vv"], 1)
    vv_rr = p.inc("vv")

    @p.on("scalar")
    def f(e):
        W(e, "vv", vv_rr)
        e.activation(p0sin[:], p0sin[:], AF.Sin).then_inc(SEM["sc"], 1)
        e.activation(ctm[:], ctm[:], AF.Silu).then_inc(SEM["sc"], 1)
    sc_sin = p.inc("sc")
    sc_silu = p.inc("sc")

    LATE["sxa"] = p.cnt["sxa"]
    @p.on("sync")
    def f(e, w=vv_rr):
        W(e, "vv", w)
        e.dma_start(out=p0n[:], in_=ncos_in).then_inc(SEM["sxa"], 16)
    sx_nc = p.inc("sxa", 16)

    @p.on("vector")
    def f(e):
        W(e, "sxa", sx_nc)
        W(e, "sc", sc_sin)
        e.scalar_tensor_tensor(p0a[:], p0n[:], -TWO_PI, p0b[:], op0=ALU.mult, op1=ALU.add)
        e.drain()
        e.tensor_scalar(p0a[:], p0a[:], PI / 2, None, op0=ALU.add)
        e.drain()
        e.tensor_scalar(p0a[:], p0a[:], PI, -PI, op0=ALU.min, op1=ALU.max)
        e.drain()
        e.drain().then_inc(SEM["vv"], 1)
    vv_rc = p.inc("vv")

    @p.on("scalar")
    def f(e):
        W(e, "vv", vv_rc)
        e.activation(p0a[:], p0a[:], AF.Sin).then_inc(SEM["sc"], 1)
    sc_cos = p.inc("sc")

    for isC in (False, True):
        for ch in range(4):
            @p.on("tensor")
            def f(e, isC=isC, ch=ch, wv=vv_rc, ws=sc_cos, g=p.cnt["vv"]):
                W(e, "sc", ws)
                W(e, "vv", g)
                if isC:
                    e.matmul(ptr0[0:128, 0:512], selc_s[:, 0:128],
                             p0a[:, ch * 512:(ch + 1) * 512], start=True, stop=False)
                    e.matmul(ptr0[0:128, 0:512], sel1_s[:, 0:128],
                             onesf[0:1, 0:512], start=False, stop=True)
                else:
                    e.matmul(ptr0[0:128, 0:512], sels_s[:, 0:128],
                             p0sin[:, ch * 512:(ch + 1) * 512], start=True, stop=True)
                e.drain().then_inc(SEM["tv"], 1)
            tv_m = p.inc("tv")

            @p.on("vector")
            def f(e, isC=isC, ch=ch, tv_m=tv_m):
                W(e, "tv", tv_m)
                dst = ropeC if isC else ropeS
                e.tensor_copy(dst[:, ch * 512:(ch + 1) * 512], ptr0[0:128, 0:512])
                e.drain()
                e.drain().then_inc(SEM["vv"], 1)
            p.inc("vv")

    @p.on("tensor")
    def f(e, gv=p.cnt["vv"]):
        W(e, "sc", sc_silu)
        W(e, "sxa", lambda: LATE["sxa"])
        W(e, "vv", gv)
        for kc in range(DC):
            e.transpose(ptr0[0:128, kc * 2:kc * 2 + 2],
                        ctm[:, kc * 128:(kc + 1) * 128], ident[0:2, 0:2])
        e.drain().then_inc(SEM["tv"], 1)
    tv_ctr = p.inc("tv")

    @p.on("vector")
    def f(e):
        W(e, "tv", tv_ctr)
        e.tensor_copy(scfm[:], ptr0[0:128, 0:2 * DC])
        e.tensor_copy(scfm32[:], ptr0[0:128, 0:2 * DC])
        e.drain().then_inc(SEM["vv"], 1)
    vv_scfm = p.inc("vv")
    p.free_at("pmod", "vv", vv_scfm)

    last_mm = {}
    last_cp = {}
    mt_free = {}
    ptr_free = {}
    modg_sx = 0
    for t in range(24):
        q = t // 4
        hi = q in (2, 5)
        sl = t % 2
        buf = wtF[sl] if hi else wt[sl]
        ssem = (("cs0" if sl == 0 else "cs1") if hi
                else ("ws0" if sl == 0 else "ws1"))
        eng = "sync" if hi else "gpsimd"

        @p.on(eng)
        def f(e, t=t, buf=buf, ssem=ssem, g=last_mm.get((hi, sl), 0)):
            W(e, "tv", g)
            e.dma_start(out=buf[:].rearrange("p (c f) -> p c f", f=512),
                        in_=wmod[:, t * 512:(t + 1) * 512]
                        .rearrange("(c p) f -> p c f", p=128)).then_inc(SEM[ssem], 16)
        wtgt = p.inc(ssem, 16)
        if not hi:
            emit_pc(1)

        @p.on("tensor")
        def f(e, sl=sl, hi=hi, ssem=ssem, wtgt=wtgt, buf=buf, vtgt=vv_scfm,
              fr=last_cp.get(sl, 0)):
            W(e, ssem, wtgt)
            W(e, "vv", max(vtgt, fr))
            src = scfm32 if hi else scfm
            for kc in range(DC):
                e.matmul(pmod2[sl][0:2, 0:512], src[:, kc * 2:kc * 2 + 2],
                         buf[:, kc * 512:(kc + 1) * 512],
                         start=(kc == 0), stop=(kc == DC - 1))
            e.drain().then_inc(SEM["tv"], 1)
        last_mm[(hi, sl)] = p.inc("tv")

        @p.on("vector")
        def f(e, sl=sl, g=last_mm[(hi, sl)], mf=mt_free.get(sl, 0)):
            W(e, "tv", max(g, mf))
            e.tensor_copy(mtmp[sl][:], pmod2[sl][0:2, 0:512])
            e.drain().then_inc(SEM["vv"], 1)
        last_cp[sl] = p.inc("vv")

        @p.on("tensor")
        def f(e, sl=sl, cp_=last_cp[sl], pf=ptr_free.get(sl, 0)):
            W(e, "vv", max(cp_, pf))
            for c4 in range(4):
                e.transpose(pmod[0:128, sl * 8 + c4 * 2:sl * 8 + c4 * 2 + 2],
                            mtmp[sl][:, c4 * 128:(c4 + 1) * 128], ident[0:2, 0:2])
            e.drain().then_inc(SEM["tv"], 1)
        mt_free[sl] = tv_tr = p.inc("tv")

        @p.on("vector")
        def f(e, t=t, q=q, sl=sl, hi=hi, tvw=tv_tr, sxw=modg_sx):
            W(e, "tv", tvw)
            if hi:
                W(e, "sx", sxw)
                e.tensor_copy(modg[:], pmod[0:128, sl * 8:sl * 8 + 8])
                e.drain()
                e.tensor_scalar(modsc[:], modg[:], 0.0, None, op0=ALU.is_ge)
                e.drain()
                e.tensor_scalar(modsc[:], modsc[:], 2.0, -1.0, op0=ALU.mult, op1=ALU.add)
                e.drain()
                e.tensor_tensor(modg[:], modg[:], modsc[:], op=ALU.mult)
                e.drain()
                e.tensor_scalar(modg[:], modg[:], 0.25, None, op0=ALU.max)
                e.drain()
                e.tensor_tensor(modg[:], modg[:], modsc[:], op=ALU.mult)
                e.drain()
            else:
                qi = {0: 0, 1: 1, 3: 2, 4: 3}[q]
                for c4 in range(4):
                    chunk = (t * 4 + c4) % DC
                    co = sl * 8 + c4 * 2
                    for s_ in range(NSEQ):
                        col = s_ * 4 * DC + qi * DC + chunk
                        if qi in (1, 3):
                            e.tensor_scalar(modv[:, col:col + 1],
                                            pmod[0:128, co + s_:co + s_ + 1],
                                            1.0, None, op0=ALU.add)
                        else:
                            e.tensor_copy(modv[:, col:col + 1],
                                          pmod[0:128, co + s_:co + s_ + 1])
                e.drain()
            e.drain().then_inc(SEM["vv"], 1)
        ptr_free[sl] = last_ev = p.inc("vv")

        if hi:
            @p.on("sync")
            def f(e, t=t, q=q, vvw=last_ev):
                W(e, "vv", vvw)
                for c4 in range(4):
                    chunk = (t * 4 + c4) % DC
                    for s_ in range(NSEQ):
                        row = s_ * 2 + (0 if q == 2 else 1)
                        e.dma_start(out=scr_rows[row:row + 1,
                                                 chunk * 128:(chunk + 1) * 128],
                                    in_=modg[:, c4 * 2 + s_:c4 * 2 + s_ + 1])\
                         .then_inc(SEM["sx"], 16)
            modg_sx = p.inc("sx", 16 * 8)

    barrier()
    d = DBG("modv", (128, NSEQ * 4 * DC))
    if d is not None:
        @p.on("sync")
        def f(e, d=d):
            e.dma_start(out=d, in_=modv[:]).then_inc(SEM["so"], 16)
        p.inc("so", 16)
    d = DBG("rows", (4, 2048))
    if d is not None:
        @p.on("sync")
        def f(e, d=d):
            e.dma_start(out=d, in_=scr_rows[0:4, 0:2048]).then_inc(SEM["so"], 16)
        p.inc("so", 16)
    d = DBG("rope", (128, 2 * S))
    if d is not None:
        rpt = p0.enter_context(nc.sbuf_tensor("rpt", [128, 2 * S], F32))
        @p.on("vector")
        def f(e, w=p.cnt["vv"] + 1):
            e.tensor_copy(rpt[:, 0:S], ropeC[:])
            e.tensor_copy(rpt[:, S:2 * S], ropeS[:])
            e.drain().then_inc(SEM["vv"], 1)
        p.inc("vv")
        @p.on("sync")
        def f(e, d=d, rpt=rpt, w=p.cnt["vv"]):
            W(e, "vv", w)
            e.dma_start(out=d, in_=rpt[:]).then_inc(SEM["so"], 16)
        p.inc("so", 16)
    p0.close()
    RET = dict(dbg=dbg_t, din=din, out=out, nc=nc, p=p, ctx=ctx,
               scr=dict(q=scr_q, xmid=scr_xmid, core=scr_core, rows=scr_rows,
                        c10=scr_c10))
    cstage = None
    RET["sb"] = dict(ident=ident, identb=identb, permb=permb, masksb=masksb,
                     ropeC=ropeC, ropeS=ropeS, onesb=onesb, onesf=onesf,
                     cpool=cpool, modv=modv, bqkv_fm=bqkv_fm, bkfm=bkfm,
                     dbfm=dbfm, sb1fm=sb1fm, ln2ab=ln2ab, bvB=bvB, boutr=boutr,
                     ubcbr=ubcbr, bcastA=bcastA, rbadj=rbadj,
                     rwsb=rwsb, cstage=cstage,
                     gb1sb=gb1sb, b2s=b2s, sb2r=sb2r)
    RET["weights"] = dict(wqkv=wqkv, wout=wout_w, down=down_w, up=up_w,
                          core=core_w, sh1=shared_w1, sh2=shared_w2,
                          ew1=e_w1, eb1=e_b1, ew2=e_w2, eb2=e_b2,
                          x=x_in)
    RET["helpers"] = (W, WG, barrier, DBG, SB)
    RET["precast"] = (emit_pc, lambda: len(PRECAST), PC_TOTAL)
    RET["sb12"] = sb12
    RET["scr"]["e1b"], RET["scr"]["e2b"] = scr_e1b, scr_e2b
    RET["scr"]["s1b"], RET["scr"]["s2b"] = scr_s1b, scr_s2b
    RET["LATE"] = LATE
    if "p1" in PH:
        add_p12(p, SEM, RET, PH)
    if "p345" in PH:
        add_p345(p, SEM, RET)

    block = ctx.enter_context(nc.Block())
    block.sync(lambda e: [fn(e) for fn in p.q["sync"]])
    block.gpsimd(lambda e: [fn(e) for fn in p.q["gpsimd"]])
    block.vector(lambda e: [fn(e) for fn in p.q["vector"]])
    block.scalar(lambda e: [fn(e) for fn in p.q["scalar"]])
    def run_tensor(e):
        pe = TProxy(e)
        for fn in p.q["tensor"]:
            fn(pe)
    block.tensor(run_tensor)
    return nc, RET


from concourse.bass_utils import run_bass_kernel_spmd


def _core_inputs(inputs, core, con):
    m = {
        "x": np.ascontiguousarray(inputs["L0_attn_x"][core * 2:core * 2 + 2]),
        "c": np.ascontiguousarray(inputs["L0_c"][core * 2:core * 2 + 2]),
        "Wqkv": inputs["L0_attn_Wqkv"], "bqkv": inputs["L0_attn_bqkv"],
        "Wout": inputs["L0_attn_Wout"], "bout": inputs["L0_attn_bout"],
        "down_w": inputs["L0_ffn_down_w"], "down_b": inputs["L0_ffn_down_b"],
        "up_w": inputs["L0_ffn_up_w"], "up_b": inputs["L0_ffn_up_b"],
        "router_w": inputs["L0_ffn_router_w"], "router_b": inputs["L0_ffn_router_b"],
        "shared_w1": inputs["L0_ffn_shared_w1"], "shared_b1": inputs["L0_ffn_shared_b1"],
        "shared_w2": inputs["L0_ffn_shared_w2"], "shared_b2": inputs["L0_ffn_shared_b2"],
        "core_w": inputs["L0_ffn_core_w"], "core_b": inputs["L0_ffn_core_b"],
        "wmod": inputs["L0_wmod"],
    }
    for e in range(8):
        for part in ("w1", "b1", "w2", "b2"):
            m[f"e{e}_{part}"] = inputs[f"L0_ffn_e{e}_{part}"]
    m.update(con)
    return m


def kernel(**inputs):
    con = host_constants()
    nc, R = build()
    in_maps = [_core_inputs(inputs, c, con) for c in range(8)]
    res = run_bass_kernel_spmd(nc, in_maps, core_ids=list(range(8)))
    out = np.concatenate([res.results[c]["out"] for c in range(8)], axis=0)
    return out.reshape(16, 2048, 2048)
